# version 14 (final)
# baseline (speedup 1.0000x reference)
import os
import numpy as np
import ml_dtypes
import concourse.bass as bass
import concourse.mybir as mybir
from concourse.bass_utils import run_bass_kernel_spmd
from contextlib import ExitStack

F32 = mybir.dt.float32
BF16 = mybir.dt.bfloat16
I32 = mybir.dt.int32
U8 = mybir.dt.uint8
OP = mybir.AluOpType
AF = mybir.ActivationFunctionType
AX = mybir.AxisListType

D = 2048
KC = 16
HID = 5632
HC = 44
NH = 16
W = 256
T_OWN = 1024
T_WIN = 1280
SCALE = float(128) ** -0.5
NEG = -1.0e30

O_MISC = 0
O_T1 = 3072
O_A = 13312
O_B = 95232
O_C = 136192

O_CS1 = O_T1
O_CS2 = O_T1 + 5120
O_RSTD2B = O_T1
O_B2B = O_T1 + 4096

O_KROT = O_A
O_QROT = O_A + 40960
O_WKBUF = O_A + 73728
O_UPLO = O_A

O_H = O_B
O_ATTN = O_B
O_UPHI = O_B
O_H2 = O_B + 8192

O_WMODBUF = O_C
O_MODSB = O_C + 16384
O_XB = O_C + 32768
O_XSQ = O_C + 37888
O_RSTDB = O_C + 43008
O_BB = O_C + 48128
O_SROWS = O_C + 53248
O_HTMP = O_C + 58368
O_RSA = O_C + 63488
O_VTM = O_C
O_WVBUF = O_C + 40960
O_MSK = O_C + 40960
O_EXPN = O_C + 57344
O_ATSB = O_C + 65536
O_MASKS = 3072
O_X2B = O_C
O_X2SQ = O_C + 2048
O_X2BUF = O_C + 4096
O_XREBUF = O_C + 12288
O_X2RE0 = O_C + 20480
O_X2RE1 = O_C + 57344
O_H2SCR = O_C + 12288
O_DSCR = O_C + 16384
O_WUBUF = O_C
O_WDBUF = O_C + 24576
O_OUTBUF = O_C + 47104
O_GTMP = O_C + 63488
assert O_GTMP + 4096 <= O_C + 69808
CAP = 206000

O_IDB = 0
O_IDF = 256
O_SWAP = 768
O_ONESB = 1024
O_ONESF = 1056
O_CVAL = 1568
O_CREP = 1600
O_SILC = 1856
O_MODT = 1984
O_SG = 2368
O_SU = 2560
O_SD = 2752
O_SDGB = 2816
O_TINY = 2880
assert O_TINY + 64 <= O_T1

CV_N128, CV_EPS, CV_NHALF = 0, 1, 2
PHASES = int(os.environ.get("MK_PHASES", "99"))


def ts_slices(total, step=512):
    out, o = [], 0
    while o < total:
        out.append((o, min(step, total - o)))
        o += step
    return out


class Sem:
    def __init__(self, nc, es, name):
        self.h = es.enter_context(nc.semaphore(name))
        self.n = 0

    def inc(self, inst, k=1):
        inst.then_inc(self.h, k)
        self.n += k
        return self.n


class DStream:

    def __init__(self, nc, es, name):
        self.sems = [Sem(nc, es, name + "0"), Sem(nc, es, name + "1")]
        self.i = 0

    def issue(self, inst):
        s = self.sems[self.i % 2]
        s.inc(inst, 16)
        self.i += 1
        return (s.h, s.n)


def build_nc():
    nc = bass.Bass("TRN2", target_bir_lowering=False, detect_race_conditions=False)

    d_x = nc.dram_tensor("x_in", [KC, 128, T_WIN], F32, kind="ExternalInput").ap()
    d_wq = nc.dram_tensor("wq_m", [KC, 128, D], F32, kind="ExternalInput").ap()
    d_wk = nc.dram_tensor("wk_m", [KC, 128, D], F32, kind="ExternalInput").ap()
    d_wv = nc.dram_tensor("wv_k", [KC, 128, D], F32, kind="ExternalInput").ap()
    d_wo = nc.dram_tensor("wo_m", [KC, 128, D], F32, kind="ExternalInput").ap()
    d_wg = nc.dram_tensor("wg_m", [HC, 128, D], U8, kind="ExternalInput").ap()
    d_wu = nc.dram_tensor("wu_m", [HC, 128, D], U8, kind="ExternalInput").ap()
    d_wd = nc.dram_tensor("wd_m", [KC, 128, HID], U8, kind="ExternalInput").ap()
    d_wmod = nc.dram_tensor("wmod_k", [KC, 128, 6 * D], F32, kind="ExternalInput").ap()
    d_crep = nc.dram_tensor("c_rep", [128, 64], F32, kind="ExternalInput").ap()
    d_sg = nc.dram_tensor("s_gate_t", [128, HC], F32, kind="ExternalInput").ap()
    d_su = nc.dram_tensor("s_up_t", [128, HC], F32, kind="ExternalInput").ap()
    d_sd = nc.dram_tensor("s_down_t", [128, KC], F32, kind="ExternalInput").ap()
    d_cs1 = nc.dram_tensor("cs1", [128, T_WIN], F32, kind="ExternalInput").ap()
    d_cs2 = nc.dram_tensor("cs2", [128, T_WIN], F32, kind="ExternalInput").ap()
    d_masks = nc.dram_tensor("masks", [2, 128, 1024], F32, kind="ExternalInput").ap()
    d_idb = nc.dram_tensor("id_bf", [128, 128], BF16, kind="ExternalInput").ap()
    d_idf = nc.dram_tensor("id_f", [128, 128], F32, kind="ExternalInput").ap()
    d_swap = nc.dram_tensor("swap_bf", [128, 128], BF16, kind="ExternalInput").ap()
    d_onesb = nc.dram_tensor("ones_b", [128, 1], BF16, kind="ExternalInput").ap()
    d_onesf = nc.dram_tensor("ones_f", [1, 128], F32, kind="ExternalInput").ap()
    d_cval = nc.dram_tensor("cvals", [128, 8], F32, kind="ExternalInput").ap()

    d_y = nc.dram_tensor("y", [KC, 128, T_OWN], F32, kind="ExternalOutput").ap()
    d_x2 = nc.dram_tensor("x2_scratch", [KC, 128, T_OWN], F32, kind="Internal").ap()

    _arena = nc.alloc_sbuf_tensor("manual_arena", [128, CAP], mybir.dt.uint8)
    SBUF_BASE = nc.lookup_mloc(_arena).addr

    def AT(name, shape, dtype, off):
        return nc.alloc_sbuf_tensor_at(name, shape, dtype, offset=SBUF_BASE + off).ap()

    t_x = AT("x", [128, KC * T_WIN], F32, O_A)
    t_h = AT("h", [128, KC * T_WIN], BF16, O_H)
    t_krot = AT("krot", [128, NH * T_WIN], BF16, O_KROT)
    t_qrot = AT("qrot", [128, NH * T_OWN], BF16, O_QROT)
    t_vtm = AT("vtm", [128, 10 * D], BF16, O_VTM)
    t_attn = AT("attn_fm", [128, NH * T_OWN], BF16, O_ATTN)
    t_uplo = AT("up_lo", [128, 40 * T_OWN], BF16, O_UPLO)
    t_uphi = AT("up_hi", [128, 4 * T_OWN], BF16, O_UPHI)
    t_h2 = AT("h2", [128, KC * T_OWN], BF16, O_H2)

    t_wmodbuf = [AT(f"wmodbuf{i}", [128, 4096], BF16, O_WMODBUF + i * 8192) for i in range(2)]
    t_wmq = [AT(f"wmq{i}", [128, 4096], BF16, O_QROT + i * 8192) for i in range(4)]
    t_modsb = AT("mod_sb", [4, 4096], F32, O_MODSB)
    t_xb = [AT(f"xb{i}", [128, T_WIN], BF16, O_XB + i * 2560) for i in range(2)]
    t_xsq = [AT(f"xsq{i}", [128, T_WIN], BF16, O_XSQ + i * 2560) for i in range(2)]
    t_rstdb = AT("rstd_b", [128, T_WIN], F32, O_RSTDB)
    t_bb = AT("b_b", [128, T_WIN], F32, O_BB)
    t_rmu = AT("r_mu", [1, T_WIN], F32, O_SROWS)
    t_rvar = AT("r_var", [1, T_WIN], F32, O_XSQ)
    t_rstd = AT("r_std", [1, T_WIN], F32, O_XB)
    t_htmp = AT("htmp", [128, T_WIN], F32, O_HTMP)
    t_wkbuf = [AT(f"wkbuf{i}", [128, D], BF16, O_WKBUF + i * 4096) for i in range(2)]
    t_wubuf = [AT(f"wubuf{i}", [128, D], BF16, O_WUBUF + i * 4096) for i in range(2)]
    t_rsa = AT("rope_a", [128, 512], F32, O_RSA)
    t_wvbuf = AT("wvbuf", [128, KC * 512], BF16, O_WVBUF)
    t_wvbuf2a = AT("wvbuf2a", [128, 12 * 512], BF16, O_C + 57344)
    t_gscr = AT("gscr", [4, 1024], F32, O_X2BUF)
    t_masks = AT("masks_sb", [128, 2 * 1024], F32, O_MASKS)
    t_msk2 = [AT(f"msk2_{p}", [128, 2048], F32, O_MSK + p * 8192) for p in range(2)]
    t_expn2 = [AT(f"expn2_{p}", [128, 2048], BF16, O_EXPN + p * 4096) for p in range(2)]
    t_atsb2 = [AT(f"atsb2_{h}", [128, 1024], BF16, O_ATSB + h * 2048) for h in range(2)]
    t_x2b = AT("x2b", [128, T_OWN], BF16, O_X2B)
    t_x2sq = AT("x2sq", [128, T_OWN], BF16, O_X2SQ)
    t_x2buf = [AT(f"x2buf{i}", [128, T_OWN], F32, O_X2BUF + i * 4096) for i in range(2)]
    t_xrebuf = [AT(f"xrebuf{i}", [128, T_OWN], F32, O_XREBUF + i * 4096) for i in range(2)]
    t_x2re = [AT("x2re0", [128, T_OWN], F32, O_X2RE0),
              AT("x2re1", [128, T_OWN], F32, O_X2RE1)]
    t_h2scr = AT("h2scr", [128, T_OWN], F32, O_H2SCR)
    t_dscr = AT("dscr", [128, T_OWN], F32, O_DSCR)
    t_wdbuf = [AT(f"wdbuf{i}", [128, HID], BF16, O_WDBUF + i * 11264) for i in range(2)]
    t_outbuf = [AT(f"outbuf{i}", [128, T_OWN], F32, O_OUTBUF + i * 4096) for i in range(2)]
    t_gtmp = [AT(f"gtmp{i}", [128, T_OWN], BF16, O_GTMP + i * 2048) for i in range(2)]

    t_cs1 = AT("cs1_sb", [128, T_WIN], F32, O_CS1)
    t_cs2 = AT("cs2_sb", [128, T_WIN], F32, O_CS2)
    t_rstd2b = AT("rstd2_b", [128, T_OWN], F32, O_RSTD2B)
    t_b2b = AT("b2_b", [128, T_OWN], F32, O_B2B)

    t_idb = AT("id_bf_sb", [128, 128], BF16, O_IDB)
    t_idf = AT("id_f_sb", [128, 128], F32, O_IDF)
    t_swap = AT("swap_sb", [128, 128], BF16, O_SWAP)
    t_onesb = AT("ones_b_sb", [128, 1], BF16, O_ONESB)
    t_onesf = AT("ones_f_sb", [1, 128], F32, O_ONESF)
    t_cval = AT("cvals_sb", [128, 8], F32, O_CVAL)
    t_crep = AT("c_rep_sb", [128, 64], F32, O_CREP)
    t_silc = AT("silu_c", [128, 64], BF16, O_SILC)
    t_modt = AT("modT", [128, 96], F32, O_MODT)
    t_sg = AT("sg_sb", [128, HC], F32, O_SG)
    t_su = AT("su_sb", [128, HC], F32, O_SU)
    t_sd = AT("sd_sb", [128, KC], F32, O_SD)
    t_sdgb = AT("sdgb", [128, KC], F32, O_SDGB)
    t_tiny = AT("tiny", [128, 24], F32, O_TINY)

    es = ExitStack()
    sy = Sem(nc, es, "sy")
    sg = Sem(nc, es, "sgd")
    st_t = Sem(nc, es, "st_t")
    st_v = Sem(nc, es, "st_v")
    st_s = Sem(nc, es, "st_s")
    s_out = Sem(nc, es, "s_out")
    s_ev = Sem(nc, es, "s_attev")
    f_wmod = Sem(nc, es, "f_wmod")
    f_wblk = Sem(nc, es, "f_wblk")
    f_wv = Sem(nc, es, "f_wv")
    f_wd = Sem(nc, es, "f_wd")
    f_xre = Sem(nc, es, "f_xre")
    f_x2re = Sem(nc, es, "f_x2re")

    dmod = DStream(nc, es, "dmod")
    dblk = DStream(nc, es, "dblk")
    dwd = DStream(nc, es, "dwd")
    dxre = DStream(nc, es, "dxre")
    dsp = DStream(nc, es, "dsp")
    dx2re = DStream(nc, es, "dx2re")
    dout = DStream(nc, es, "dout")
    s_wv = Sem(nc, es, "s_wv")
    s_msk = Sem(nc, es, "s_msk")

    mark = {}

    def WT(eng, mk):
        eng.wait_ge(mk[0], mk[1])

    def hap(c, o, n):
        return t_h[:, c * T_WIN + o: c * T_WIN + o + n]

    def h2ap(c, o, n):
        return t_h2[:, c * T_OWN + o: c * T_OWN + o + n]

    def upap(m, o, n):
        if m < 40:
            return t_uplo[:, m * T_OWN + o: m * T_OWN + o + n]
        return t_uphi[:, (m - 40) * T_OWN + o: (m - 40) * T_OWN + o + n]

    def wmbuf(pi, k):
        if pi == 1 and 2 <= k < 6:
            return t_wmq[k - 2]
        return t_wmodbuf[k % 2]

    def wvsl(np_, k):
        if np_ % 2 and k < 12:
            return t_wvbuf2a[:, k * 512:(k + 1) * 512]
        return t_wvbuf[:, k * 512:(k + 1) * 512]

    with es, nc.Block() as block:
        t_dumpA = AT("dumpA", [128, 16 * 1024], F32, O_A)
        t_dumpB = AT("dumpB", [128, 10 * 1024], F32, O_B)
        t_dumpC = AT("dumpC", [128, 6 * 1024], F32, O_C)

        def emit_tail():
            dump_mode = os.environ.get("MK_DUMP", "A")
            simsafe = os.environ.get("MK_SIM", "") == "1"

            @block.sync
            def _(y):
                y.wait_ge(st_t.h, st_t.n)
                y.wait_ge(st_v.h, st_v.n)
                y.wait_ge(st_s.h, st_s.n)
                y.wait_ge(s_ev.h, s_ev.n)
                for m in range(KC):
                    if simsafe:
                        srcap = t_cs1[:, 0:1024] if PHASES < 9 else t_masks[:, 0:1024]
                    elif dump_mode == "A":
                        srcap = t_dumpA[:, m * 1024:(m + 1) * 1024]
                    else:
                        srcap = (t_dumpB[:, m * 1024:(m + 1) * 1024] if m < 10
                                 else t_dumpC[:, (m - 10) * 1024:(m - 9) * 1024])
                    mark[f"yout{m}"] = dout.issue(y.dma_start(d_y[m], srcap))
                y.wait_ge(dout.sems[0].h, dout.sems[0].n)
                y.wait_ge(dout.sems[1].h, dout.sems[1].n)

        @block.sync
        def _(y):
            for c in range(KC):
                sy.inc(y.dma_start(t_x[:, c * T_WIN:(c + 1) * T_WIN], d_x[c]), 16)
            for dst, src in [(t_idb, d_idb), (t_idf, d_idf), (t_swap, d_swap),
                             (t_onesb, d_onesb), (t_onesf, d_onesf), (t_cval, d_cval),
                             (t_crep, d_crep), (t_sg, d_sg), (t_su, d_su), (t_sd, d_sd),
                             (t_cs1, d_cs1), (t_cs2, d_cs2)]:
                sy.inc(y.dma_start(dst[:, :], src), 16)
            mark["consts"] = sy.n

        @block.gpsimd
        def _(g):
            if PHASES < 2:
                return
            for k in range(KC):
                if k >= 2:
                    g.wait_ge(f_wmod.h, k - 1)
                mark[f"wmodA{k}"] = dmod.issue(g.dma_start(
                    t_wmodbuf[k % 2][:, :], d_wmod[k][:, 0:4096], max_dma_last_dim=2048))

        stats_t_base = st_t.n

        @block.scalar
        def _(s):
            s.wait_ge(sy.h, mark["consts"])
            st_s.inc(s.activation(out=t_silc[:, :], in_=t_crep[:, :], func=AF.Silu))
            mark["silc"] = st_s.n
            s.wait_ge(sy.h, mark["consts"])
            for c in range(KC):
                if c >= 2:
                    s.wait_ge(st_t.h, stats_t_base + c - 1)
                st_s.inc(s.activation(out=t_xsq[c % 2][:, :],
                                      in_=t_x[:, c * T_WIN:(c + 1) * T_WIN], func=AF.Square))
                mark[f"xsq{c}"] = st_s.n

        @block.vector
        def _(v):
            v.wait_ge(sy.h, mark["consts"])
            for c in range(KC):
                if c >= 2:
                    v.wait_ge(st_t.h, stats_t_base + c - 1)
                st_v.inc(v.tensor_copy(t_xb[c % 2][:, :], t_x[:, c * T_WIN:(c + 1) * T_WIN]))
                mark[f"xb{c}"] = st_v.n

        es_p = ExitStack()
        ps_sum = [es_p.enter_context(nc.psum_tensor(f"ps_sum{i}", [1, 512], F32)) for i in range(3)]
        ps_sq = [es_p.enter_context(nc.psum_tensor(f"ps_sq{i}", [1, 512], F32)) for i in range(3)]

        @block.tensor
        def _(t):
            t.wait_ge(sy.h, mark["consts"])
            for c in range(KC):
                t.wait_ge(st_v.h, mark[f"xb{c}"])
                t.wait_ge(st_s.h, mark[f"xsq{c}"])
                last = None
                for i, (o, n) in enumerate(ts_slices(T_WIN)):
                    t.matmul(ps_sum[i][:, 0:n], t_onesb[:, :], t_xb[c % 2][:, o:o + n],
                             start=(c == 0), stop=(c == KC - 1))
                    last = t.matmul(ps_sq[i][:, 0:n], t_onesb[:, :], t_xsq[c % 2][:, o:o + n],
                                    start=(c == 0), stop=(c == KC - 1))
                st_t.inc(last)
            assert st_t.n == stats_t_base + KC
            mark["stats_mm"] = st_t.n

        @block.vector
        def _(v):
            v.wait_ge(st_t.h, mark["stats_mm"])
            for i, (o, n) in enumerate(ts_slices(T_WIN)):
                v.tensor_copy(t_rmu[:, o:o + n], ps_sum[i][:, 0:n])
                v.tensor_copy(t_rvar[:, o:o + n], ps_sq[i][:, 0:n])
            v.tensor_scalar(out=t_rmu[:, :], in0=t_rmu[:, :],
                            scalar1=1.0 / 2048.0, scalar2=None, op0=OP.mult)
            v.tensor_tensor(t_rstd[:, :], t_rmu[:, :], t_rmu[:, :], OP.mult)
            st_v.inc(v.scalar_tensor_tensor(out=t_rvar[:, :], in0=t_rvar[:, :],
                                            scalar=1.0 / 2048.0, in1=t_rstd[:, :],
                                            op0=OP.mult, op1=OP.subtract))
            mark["var_row"] = st_v.n

        @block.scalar
        def _(s):
            s.wait_ge(st_v.h, mark["var_row"])
            s.activation(out=t_rstd[:, :], in_=t_rvar[:, :], func=AF.Ln,
                         bias=t_cval[0:1, CV_EPS:CV_EPS + 1], scale=1.0)
            st_s.inc(s.activation(out=t_rstd[:, :], in_=t_rstd[:, :], func=AF.Exp,
                                  bias=0.0, scale=t_cval[0:1, CV_NHALF:CV_NHALF + 1]))
            mark["rstd_row"] = st_s.n

        es_bc = ExitStack()
        ps_bc = [es_bc.enter_context(nc.psum_tensor(f"ps_bc{i}", [128, 512], F32)) for i in range(2)]
        bc_t_base = st_t.n
        bc_v_base = st_v.n

        @block.tensor
        def _(t):
            t.wait_ge(st_s.h, mark["rstd_row"])
            for j, row in enumerate([t_rstd, t_rmu]):
                for i, (o, n) in enumerate(ts_slices(T_WIN)):
                    idx = j * 3 + i
                    if idx >= 2:
                        t.wait_ge(st_v.h, bc_v_base + idx - 1)
                    st_t.inc(t.matmul(ps_bc[idx % 2][:, 0:n], t_onesf[:, :],
                                      row[:, o:o + n], start=True, stop=True))

        @block.vector
        def _(v):
            for i, (o, n) in enumerate(ts_slices(T_WIN)):
                v.wait_ge(st_t.h, bc_t_base + i + 1)
                st_v.inc(v.tensor_copy(t_rstdb[:, o:o + n], ps_bc[i % 2][:, 0:n]))
            for i, (o, n) in enumerate(ts_slices(T_WIN)):
                idx = 3 + i
                v.wait_ge(st_t.h, bc_t_base + idx + 1)
                st_v.inc(v.scalar_tensor_tensor(out=t_bb[:, o:o + n], in0=ps_bc[idx % 2][:, 0:n],
                                                scalar=-1.0, in1=t_rstdb[:, o:o + n],
                                                op0=OP.mult, op1=OP.mult))
            mark["bcast_done"] = st_v.n
        es_bc.close()
        es_p.close()

        if PHASES < 2:
            emit_tail()
            return nc

        def emit_mod_pass(pi, wmark_prefix, bank_wait_mark):
            es_m = ExitStack()
            ps_mod = [es_m.enter_context(nc.psum_tensor(f"ps_mod{pi}_{i}", [4, 512], F32))
                      for i in range(8)]

            @block.tensor
            def _(t):
                t.wait_ge(st_s.h, mark["silc"])
                t.wait_ge(st_v.h, mark[bank_wait_mark])
                for k in range(KC):
                    WT(t, mark[f"{wmark_prefix}{k}"])
                    last = None
                    for i in range(8):
                        last = t.matmul(ps_mod[i][:, :], t_silc[:, 4 * k:4 * k + 4],
                                        wmbuf(pi, k)[:, i * 512:(i + 1) * 512],
                                        start=(k == 0), stop=(k == KC - 1))
                    f_wmod.inc(last)
                mark[f"mod{pi}_mm"] = (f_wmod.h, f_wmod.n)

            @block.vector
            def _(v):
                WT(v, mark[f"mod{pi}_mm"])
                last = None
                for i in range(8):
                    last = v.tensor_copy(t_modsb[:, i * 512:(i + 1) * 512], ps_mod[i][:, :])
                if pi == 0:
                    last = v.tensor_scalar(out=t_modsb[:, 2048:4096], in0=t_modsb[:, 2048:4096],
                                           scalar1=1.0, scalar2=None, op0=OP.add)
                elif pi == 2:
                    last = v.tensor_scalar(out=t_modsb[:, 0:2048], in0=t_modsb[:, 0:2048],
                                           scalar1=1.0, scalar2=None, op0=OP.add)
                if pi in (1, 2):
                    base = 0 if pi == 1 else 2048
                    for hh in range(2):
                        gsl = t_modsb[:, base + hh * 1024: base + (hh + 1) * 1024]
                        scr = t_gscr[0:4, 0:1024] if pi == 2 else t_htmp[0:4, 0:1024]
                        v.tensor_scalar(out=scr, in0=gsl, scalar1=0.0, scalar2=None, op0=OP.is_lt)
                        v.tensor_scalar(out=scr, in0=scr, scalar1=-2.0, scalar2=1.0,
                                        op0=OP.mult, op1=OP.add)
                        v.tensor_tensor(gsl, gsl, scr, OP.mult)
                        v.tensor_scalar(out=gsl, in0=gsl, scalar1=0.25, scalar2=None,
                                        op0=OP.max)
                        last = v.tensor_tensor(gsl, gsl, scr, OP.mult)
                st_v.inc(last)
                mark[f"mod{pi}_adj"] = st_v.n

            es_m.close()
            es_t = ExitStack()
            ps_mt = es_t.enter_context(nc.psum_tensor(f"ps_mt{pi}", [128, 128], F32))

            @block.tensor
            def _(t):
                t.wait_ge(st_v.h, mark[f"mod{pi}_adj"])
                last = None
                for i in range(32):
                    last = t.transpose(ps_mt[:, 4 * i:4 * i + 4],
                                       t_modsb[:, i * 128:(i + 1) * 128], t_idf[0:4, 0:4])
                st_t.inc(last)
                mark[f"mod{pi}_tr"] = st_t.n

            @block.vector
            def _(v):
                v.wait_ge(st_t.h, mark[f"mod{pi}_tr"])
                src = ps_mt.ap().rearrange("p (i r) -> p i r", r=4)[:, :, 0:1]
                st_v.inc(v.tensor_copy(t_modt[:, 32 * pi:32 * pi + 32], src))
                mark[f"mod{pi}_done"] = st_v.n
            es_t.close()

        if PHASES >= 3:
            @block.vector
            def _(v):
                for c in range(KC):
                    xc = t_x[:, c * T_WIN:(c + 1) * T_WIN]
                    v.tensor_tensor(t_htmp[:, :], xc, t_rstdb[:, :], OP.mult)
                    st_v.inc(v.tensor_tensor(hap(c, 0, T_WIN), t_htmp[:, :], t_bb[:, :], OP.add))
                    mark[f"hpre{c}"] = st_v.n

        emit_mod_pass(0, "wmodA", "bcast_done")

        if PHASES < 3:
            emit_tail()
            return nc

        @block.vector
        def _(v):
            for c in range(KC):
                st_v.inc(v.tensor_scalar(out=hap(c, 0, T_WIN), in0=hap(c, 0, T_WIN),
                                         scalar1=t_modt[:, 16 + c:17 + c],
                                         scalar2=t_modt[:, c:c + 1],
                                         op0=OP.mult, op1=OP.add))
                mark[f"h{c}"] = st_v.n

        if PHASES < 4:
            emit_tail()
            return nc

        @block.gpsimd
        def _(g):
            for m in range(KC):
                if m >= 2:
                    g.wait_ge(f_wblk.h, m - 1)
                else:
                    g.wait_ge(st_v.h, mark[f"h{KC - 1}"])
                mark[f"wk{m}"] = dblk.issue(g.dma_start(
                    t_wkbuf[m % 2][:, :], d_wk[m], max_dma_last_dim=2048))
            if PHASES < 5:
                return
            for k in range(KC):
                if k < 2 or k >= 8:
                    g.wait_ge(f_wmod.h, KC + k - 1)
                elif k >= 6:
                    g.wait_ge(f_wmod.h, KC + k - 5)
                mark[f"wmodB{k}"] = dmod.issue(g.dma_start(
                    wmbuf(1, k)[:, :], d_wmod[k][:, 4096:8192], max_dma_last_dim=2048))
            if PHASES < 6:
                return
            for m in range(KC):
                g.wait_ge(f_wblk.h, KC + m - 1)
                mark[f"wq{m}"] = dblk.issue(g.dma_start(
                    t_wkbuf[m % 2][:, :], d_wq[m], max_dma_last_dim=2048))
            if PHASES < 7:
                return
            for k in range(KC):
                g.wait_ge(f_wmod.h, 2 * KC + k - 1)
                mark[f"wmodC{k}"] = dmod.issue(g.dma_start(
                    t_wmodbuf[k % 2][:, :], d_wmod[k][:, 8192:12288], max_dma_last_dim=2048))
            if PHASES < 8:
                return
            for np_ in range(4):
                if np_ >= 2:
                    g.wait_ge(f_wv.h, np_ - 1)
                for k in range(12):
                    s_wv.inc(g.dma_start(wvsl(np_, k),
                                         d_wv[k][:, np_ * 512:(np_ + 1) * 512],
                                         max_dma_last_dim=2048), 16)
                mark[f"wv{np_}a"] = (s_wv.h, s_wv.n)
                if np_ >= 1:
                    g.wait_ge(f_wv.h, np_)
                for k in range(12, KC):
                    s_wv.inc(g.dma_start(wvsl(np_, k),
                                         d_wv[k][:, np_ * 512:(np_ + 1) * 512],
                                         max_dma_last_dim=2048), 16)
                mark[f"wv{np_}"] = (s_wv.h, s_wv.n)
            if PHASES < 10:
                return
            for m in range(KC):
                g.wait_ge(f_wblk.h, 2 * KC + m - 1)
                mark[f"wo{m}"] = dblk.issue(g.dma_start(
                    t_wkbuf[m % 2][:, :], d_wo[m], max_dma_last_dim=2048))

        def emit_proj_rope(name, wprefix, n_tok, dst, dst_stride, q_off, bank_wait_mark):
            sl = ts_slices(n_tok)
            S = len(sl)
            es_k = ExitStack()
            ps_p = [[es_k.enter_context(nc.psum_tensor(f"ps_{name}p{pp}{i}", [128, 512], F32))
                     for i in range(len(sl))] for pp in range(2)]
            ps_w = [es_k.enter_context(nc.psum_tensor(f"ps_{name}w{i}", [128, 512], F32))
                    for i in range(2)]
            t_base, s_base, v_base = st_t.n, st_s.n, st_v.n
            fb = f_wblk.n

            def kraw(m, o, n):
                return t_xb[m % 2][:, o:o + n]

            @block.tensor
            def _(t):
                t.wait_ge(st_v.h, mark[bank_wait_mark])
                t.wait_ge(st_v.h, mark[f"h{KC - 1}"])
                for m in range(KC + 1):
                    if m < KC:
                        WT(t, mark[f"{wprefix}{m}"])
                        if m >= 2:
                            t.wait_ge(st_s.h, s_base + m - 1)
                        last = None
                        for k in range(KC):
                            for i, (o, n) in enumerate(sl):
                                last = t.matmul(ps_p[m % 2][i][:, 0:n],
                                                t_wkbuf[m % 2][:, k * 128:(k + 1) * 128],
                                                hap(k, q_off + o, n),
                                                start=(k == 0), stop=(k == KC - 1))
                        f_wblk.inc(last)
                        assert f_wblk.n == fb + m + 1
                    mm = m - 1
                    if mm >= 0:
                        t.wait_ge(st_s.h, s_base + mm + 1)
                        for i, (o, n) in enumerate(sl):
                            w_idx = mm * S + i
                            if w_idx >= 2:
                                t.wait_ge(st_v.h, v_base + w_idx - 1)
                            st_t.inc(t.matmul(ps_w[w_idx % 2][:, 0:n], t_swap[:, :],
                                              kraw(mm, o, n), start=True, stop=True))
                            assert st_t.n == t_base + w_idx + 1

            @block.scalar
            def _(s):
                for m in range(KC):
                    s.wait_ge(f_wblk.h, fb + m + 1)
                    if m >= 2:
                        s.wait_ge(st_t.h, t_base + (m - 1) * S)
                        s.wait_ge(st_v.h, v_base + (m - 1) * S)
                    last = None
                    for i, (o, n) in enumerate(sl):
                        last = s.activation(out=kraw(m, o, n), in_=ps_p[m % 2][i][:, 0:n],
                                            func=AF.Copy)
                    st_s.inc(last)
                    assert st_s.n == s_base + m + 1

            @block.vector
            def _(v):
                for m in range(KC):
                    for i, (o, n) in enumerate(sl):
                        w_idx = m * S + i
                        v.wait_ge(st_t.h, t_base + w_idx + 1)
                        co = q_off + o
                        v.tensor_tensor(t_rsa[:, 0:n], kraw(m, o, n), t_cs1[:, co:co + n],
                                        OP.mult)
                        v.tensor_tensor(t_htmp[:, 0:n], ps_w[w_idx % 2][:, 0:n],
                                        t_cs2[:, co:co + n], OP.mult)
                        st_v.inc(v.tensor_tensor(
                            dst[:, m * dst_stride + o: m * dst_stride + o + n],
                            t_htmp[:, 0:n], t_rsa[:, 0:n], OP.add))
                        assert st_v.n == v_base + w_idx + 1
            mark[f"{name}_done"] = st_v.n
            es_k.close()

        emit_proj_rope("K", "wk", T_WIN, t_krot, T_WIN, 0, "mod0_done")
        if PHASES < 5:
            emit_tail()
            return nc
        emit_mod_pass(1, "wmodB", "K_done")
        if PHASES < 6:
            emit_tail()
            return nc
        emit_proj_rope("Q", "wq", T_OWN, t_qrot, T_OWN, 256, "mod1_done")
        if PHASES < 7:
            emit_tail()
            return nc
        emit_mod_pass(2, "wmodC", "Q_done")
        if PHASES < 8:
            emit_tail()
            return nc

        es_v = ExitStack()
        ps_v = [es_v.enter_context(nc.psum_tensor(f"ps_v{i}", [128, 512], F32)) for i in range(2)]
        vp_t_base, vp_s_base = st_t.n, st_s.n

        @block.tensor
        def _(t):
            t.wait_ge(st_v.h, mark["mod2_done"])
            idx = 0
            for np_ in range(4):
                WT(t, mark[f"wv{np_}a"])
                for tt in range(10):
                    if idx >= 2:
                        t.wait_ge(st_s.h, vp_s_base + idx - 1)
                    last = None
                    for k in range(KC):
                        if tt == 0 and k == 12:
                            WT(t, mark[f"wv{np_}"])
                        last = t.matmul(ps_v[idx % 2][:, :], hap(k, tt * 128, 128),
                                        wvsl(np_, k),
                                        start=(k == 0), stop=(k == KC - 1))
                    if tt == 9:
                        f_wv.inc(last)
                    else:
                        st_t.inc(last)
                    idx += 1

        @block.scalar
        def _(s):
            idx = 0
            for np_ in range(4):
                for tt in range(10):
                    if tt == 9:
                        s.wait_ge(f_wv.h, np_ + 1)
                    else:
                        s.wait_ge(st_t.h, vp_t_base + np_ * 9 + tt + 1)
                    st_s.inc(s.activation(
                        out=t_vtm[:, tt * D + np_ * 512: tt * D + (np_ + 1) * 512],
                        in_=ps_v[idx % 2][:, :], func=AF.Copy))
                    idx += 1
            mark["vtm_done"] = st_s.n
        es_v.close()

        if PHASES < 9:
            emit_tail()
            return nc

        @block.sync
        def _(y):
            y.wait_ge(st_v.h, mark["mod2_done"])
            for cls in range(2):
                s_msk.inc(y.dma_start(t_masks[:, cls * 1024:(cls + 1) * 1024], d_masks[cls]), 16)
            mark["masks"] = (s_msk.h, s_msk.n)

        es_a = ExitStack()
        ps_sc = es_a.enter_context(nc.psum_tensor("ps_sc", [128, 2048], F32))
        ps_at = [es_a.enter_context(nc.psum_tensor(f"ps_at{i}", [128, 1024], BF16))
                 for i in range(3)]
        ps_av = es_a.enter_context(nc.psum_tensor("ps_av", [128, 512], F32))
        NG = 32

        tval, vval, sval, eval_ = {}, {}, {}, {}
        tb, vb_, sb_, eb_ = st_t.n, st_v.n, st_s.n, s_ev.n
        tc, vc_, sc_, ec_ = tb, vb_, sb_, eb_
        for g in range(NG + 2):
            if g < NG:
                tc += 1
                tval[("sc", g)] = tc
            if 0 <= g - 1 < NG:
                tc += 1
                tval[("tr", g - 1)] = tc
            if 0 <= g - 2 < NG:
                tc += 1
                tval[("av", g - 2)] = tc
        for g in range(NG + 2):
            if g < NG:
                vc_ += 1
                vval[("mm", g)] = vc_
            if 0 <= g - 1 < NG:
                vc_ += 1
                vval[("rn", g - 1)] = vc_
            if 0 <= g - 2 < NG:
                vc_ += 1
                vval[("c0", g - 2)] = vc_
                vc_ += 1
                vval[("c1", g - 2)] = vc_
        for g in range(NG + 2):
            if g < NG:
                sc_ += 1
                sval[("ex", g)] = sc_
            if 0 <= g - 2 < NG:
                ec_ += 1
                eval_[("e0", g - 2)] = ec_
                ec_ += 1
                eval_[("e1", g - 2)] = ec_

        @block.tensor
        def _(t):
            t.wait_ge(st_s.h, mark["vtm_done"])
            t.wait_ge(st_v.h, mark["mod2_done"])
            for g in range(NG + 2):
                if g < NG:
                    ci, j2 = divmod(g, 8)
                    vb = ci * 256
                    if g >= 1:
                        t.wait_ge(st_v.h, vval[("mm", g - 1)])
                    last = None
                    for h in range(2):
                        hd = 2 * j2 + h
                        for wt in range(2):
                            qc = hd * T_OWN + ci * 256 + wt * 128
                            last = t.matmul(ps_sc[:, (h * 2 + wt) * 512:(h * 2 + wt + 1) * 512],
                                            t_qrot[:, qc:qc + 128],
                                            t_krot[:, hd * T_WIN + vb: hd * T_WIN + vb + 512],
                                            start=True, stop=True)
                    st_t.inc(last)
                    assert st_t.n == tval[("sc", g)]
                gg = g - 1
                if 0 <= gg < NG:
                    p = gg % 2
                    t.wait_ge(st_v.h, vval[("rn", gg)])
                    last = None
                    for h in range(2):
                        u = 2 * gg + h - 3
                        if u >= 0:
                            t.wait_ge(st_v.h, vval[(f"c{u % 2}", u // 2)])
                        dst = ps_at[(2 * gg + h) % 3]
                        for vc in range(4):
                            for wt in range(2):
                                last = t.transpose(
                                    dst[:, vc * 256 + wt * 128: vc * 256 + wt * 128 + 128],
                                    t_expn2[p][:, (h * 2 + wt) * 512 + vc * 128:
                                               (h * 2 + wt) * 512 + (vc + 1) * 128],
                                    t_idb[:, :])
                    st_t.inc(last)
                    assert st_t.n == tval[("tr", gg)]
                gg = g - 2
                if 0 <= gg < NG:
                    ci, j2 = divmod(gg, 8)
                    if ("e1", gg - 1) in eval_:
                        t.wait_ge(s_ev.h, eval_[("e1", gg - 1)])
                    last = None
                    for h in range(2):
                        hd = 2 * j2 + h
                        t.wait_ge(st_v.h, vval[(f"c{h}", gg)])
                        for vc in range(4):
                            tvi = ci * 2 + vc
                            last = t.matmul(ps_av[:, h * 256:(h + 1) * 256],
                                            t_vtm[:, tvi * D + hd * 128: tvi * D + (hd + 1) * 128],
                                            t_atsb2[h][:, vc * 256:(vc + 1) * 256],
                                            start=(vc == 0), stop=(vc == 3))
                    st_t.inc(last)
                    assert st_t.n == tval[("av", gg)]

        @block.vector
        def _(v):
            WT(v, mark["masks"])
            for g in range(NG + 2):
                if g < NG:
                    ci = g // 8
                    p = g % 2
                    cls = 0 if ci == 0 else 1
                    v.wait_ge(st_t.h, tval[("sc", g)])
                    if ("ex", g - 2) in sval:
                        v.wait_ge(st_s.h, sval[("ex", g - 2)])
                    mask3 = t_masks[:, cls * 1024:(cls + 1) * 1024].unsqueeze(1).broadcast_to(
                        [128, 2, 1024])
                    v.scalar_tensor_tensor(
                        out=t_msk2[p].rearrange("a (h n) -> a h n", h=2),
                        in0=ps_sc.ap().rearrange("a (h n) -> a h n", h=2),
                        scalar=SCALE, in1=mask3, op0=OP.mult, op1=OP.add)
                    st_v.inc(v.reduce_max(t_tiny[:, p * 12: p * 12 + 4],
                                          t_msk2[p].rearrange("a (q n) -> a q n", q=4),
                                          axis=AX.X, negate=True))
                    assert st_v.n == vval[("mm", g)]
                gg = g - 1
                if 0 <= gg < NG:
                    p = gg % 2
                    v.wait_ge(st_s.h, sval[("ex", gg)])
                    for q in range(4):
                        v.reciprocal(t_tiny[:, p * 12 + 8 + q: p * 12 + 9 + q],
                                     t_tiny[:, p * 12 + 4 + q: p * 12 + 5 + q])
                    last = None
                    for q in range(4):
                        last = v.tensor_scalar(out=t_expn2[p][:, q * 512:(q + 1) * 512],
                                               in0=t_expn2[p][:, q * 512:(q + 1) * 512],
                                               scalar1=t_tiny[:, p * 12 + 8 + q: p * 12 + 9 + q],
                                               scalar2=None, op0=OP.mult)
                    st_v.inc(last)
                    assert st_v.n == vval[("rn", gg)]
                gg = g - 2
                if 0 <= gg < NG:
                    for h in range(2):
                        v.wait_ge(st_t.h, tval[("tr", gg)])
                        if ("av", gg - 1) in tval:
                            v.wait_ge(st_t.h, tval[("av", gg - 1)])
                        st_v.inc(v.tensor_copy(t_atsb2[h][:, :], ps_at[(2 * gg + h) % 3][:, :]))
                        assert st_v.n == vval[(f"c{h}", gg)]

        @block.scalar
        def _(s):
            for g in range(NG + 2):
                if g < NG:
                    p = g % 2
                    s.wait_ge(st_v.h, vval[("mm", g)])
                    if ("tr", g - 2) in tval:
                        s.wait_ge(st_t.h, tval[("tr", g - 2)])
                    for q in range(4):
                        s.activation(out=t_expn2[p][:, q * 512:(q + 1) * 512],
                                     in_=t_msk2[p][:, q * 512:(q + 1) * 512],
                                     func=AF.Exp,
                                     bias=t_tiny[:, p * 12 + q: p * 12 + q + 1],
                                     scale=1.0,
                                     accum_out=t_tiny[:, p * 12 + 4 + q: p * 12 + 5 + q])
                    st_s.inc(s.activation(out=t_tiny[:, p * 12 + 4: p * 12 + 8],
                                          in_=t_tiny[:, p * 12 + 4: p * 12 + 8],
                                          func=AF.Copy))
                    assert st_s.n == sval[("ex", g)]
                gg = g - 2
                if 0 <= gg < NG:
                    ci, j2 = divmod(gg, 8)
                    s.wait_ge(st_t.h, tval[("av", gg)])
                    for h in range(2):
                        hd = 2 * j2 + h
                        s_ev.inc(s.activation(
                            out=t_attn[:, hd * T_OWN + ci * 256: hd * T_OWN + (ci + 1) * 256],
                            in_=ps_av[:, h * 256:(h + 1) * 256], func=AF.Copy))
                        assert s_ev.n == eval_[(f"e{h}", gg)]
        mark["attn_done"] = s_ev.n
        es_a.close()

        if PHASES < 10:
            emit_tail()
            return nc

        es_o = ExitStack()
        ps_o = [[es_o.enter_context(nc.psum_tensor(f"ps_o{p}{i}", [128, 512], F32))
                 for i in range(2)] for p in range(2)]
        ps_s2 = [es_o.enter_context(nc.psum_tensor(f"ps_s2_{i}", [1, 512], F32)) for i in range(2)]
        ps_q2 = [es_o.enter_context(nc.psum_tensor(f"ps_q2_{i}", [1, 512], F32)) for i in range(2)]
        wo_t_base, wo_v_base, wo_s_base = st_t.n, st_v.n, st_s.n
        wo_fb = f_wblk.n

        @block.tensor
        def _(t):
            t.wait_ge(s_ev.h, mark["attn_done"])
            for m in range(KC):
                WT(t, mark[f"wo{m}"])
                if m >= 2:
                    t.wait_ge(st_s.h, wo_s_base + 2 * (m - 2) + 1)
                p = m % 2
                last = None
                for k in range(KC):
                    for i in range(2):
                        last = t.matmul(ps_o[p][i][:, :],
                                        t_wkbuf[m % 2][:, k * 128:(k + 1) * 128],
                                        t_attn[:, k * T_OWN + i * 512: k * T_OWN + (i + 1) * 512],
                                        start=(k == 0), stop=(k == KC - 1))
                f_wblk.inc(last)
                if m >= 1:
                    t.wait_ge(st_v.h, wo_v_base + 2 * m)
                    t.wait_ge(st_s.h, wo_s_base + 2 * m)
                    last = None
                    for i in range(2):
                        t.matmul(ps_s2[i][:, :], t_onesb[:, :],
                                 t_x2b[:, i * 512:(i + 1) * 512],
                                 start=(m == 1), stop=False)
                        last = t.matmul(ps_q2[i][:, :], t_onesb[:, :],
                                        t_x2sq[:, i * 512:(i + 1) * 512],
                                        start=(m == 1), stop=False)
                    st_t.inc(last)
                else:
                    st_t.inc(t.nop())
                assert st_t.n == wo_t_base + m + 1
            t.wait_ge(st_v.h, wo_v_base + 2 * KC)
            t.wait_ge(st_s.h, wo_s_base + 2 * KC)
            last = None
            for i in range(2):
                t.matmul(ps_s2[i][:, :], t_onesb[:, :], t_x2b[:, i * 512:(i + 1) * 512],
                         start=False, stop=True)
                last = t.matmul(ps_q2[i][:, :], t_onesb[:, :], t_x2sq[:, i * 512:(i + 1) * 512],
                                start=False, stop=True)
            st_t.inc(last)
            mark["stats2_mm"] = st_t.n

        @block.scalar
        def _(s):
            for m in range(KC):
                p = m % 2
                s.wait_ge(f_wblk.h, wo_fb + m + 1)
                last = None
                for i in range(2):
                    last = s.activation(out=t_htmp[:, i * 512:(i + 1) * 512],
                                        in_=ps_o[p][i][:, :], func=AF.Copy, bias=0.0,
                                        scale=t_modt[:, 32 + m:33 + m])
                st_s.inc(last)
                s.wait_ge(st_v.h, wo_v_base + 2 * m + 1)
                st_s.inc(s.activation(out=t_x2sq[:, :], in_=t_x2buf[p][:, :], func=AF.Square))
                assert st_s.n == wo_s_base + 2 * m + 2

        @block.sync
        def _(y):
            for m in range(KC):
                if m >= 2:
                    y.wait_ge(st_v.h, wo_v_base + 2 * (m - 2) + 1)
                else:
                    y.wait_ge(s_ev.h, mark["attn_done"])
                mark[f"xre{m}"] = dxre.issue(y.dma_start(t_xrebuf[m % 2][:, :],
                                                         d_x[m][:, 256:T_WIN]))
                if m >= 1:
                    y.wait_ge(st_v.h, wo_v_base + 2 * (m - 1) + 1)
                    mark[f"x2sp{m - 1}"] = dsp.issue(
                        y.dma_start(d_x2[m - 1], t_x2buf[(m - 1) % 2][:, :]))
            y.wait_ge(st_v.h, wo_v_base + 2 * (KC - 1) + 1)
            mark[f"x2sp{KC - 1}"] = dsp.issue(
                y.dma_start(d_x2[KC - 1], t_x2buf[(KC - 1) % 2][:, :]))

        @block.vector
        def _(v):
            for m in range(KC):
                p = m % 2
                v.wait_ge(st_s.h, wo_s_base + 2 * m + 1)
                WT(v, mark[f"xre{m}"])
                if m >= 2:
                    WT(v, mark[f"x2sp{m - 2}"])
                inst = v.tensor_tensor(t_x2buf[p][:, :], t_htmp[:, 0:T_OWN],
                                       t_xrebuf[p][:, :], OP.add)
                st_v.inc(inst)
                assert st_v.n == wo_v_base + 2 * m + 1
                if m >= 1:
                    v.wait_ge(st_t.h, wo_t_base + m)
                st_v.inc(v.tensor_copy(t_x2b[:, :], t_x2buf[p][:, :]))
                assert st_v.n == wo_v_base + 2 * m + 2

        res15_v = wo_v_base + 2 * (KC - 1) + 1

        if PHASES < 11:
            emit_tail()
            return nc

        @block.vector
        def _(v):
            v.wait_ge(st_t.h, mark["stats2_mm"])
            for i in range(2):
                v.tensor_copy(t_rmu[:, i * 512:(i + 1) * 512], ps_s2[i][:, :])
                v.tensor_copy(t_rvar[:, i * 512:(i + 1) * 512], ps_q2[i][:, :])
            v.tensor_scalar(out=t_rmu[:, 0:T_OWN], in0=t_rmu[:, 0:T_OWN],
                            scalar1=1.0 / 2048.0, scalar2=None, op0=OP.mult)
            v.tensor_tensor(t_rstd[:, 0:T_OWN], t_rmu[:, 0:T_OWN], t_rmu[:, 0:T_OWN], OP.mult)
            st_v.inc(v.scalar_tensor_tensor(out=t_rvar[:, 0:T_OWN],
                                            in0=t_rvar[:, 0:T_OWN], scalar=1.0 / 2048.0,
                                            in1=t_rstd[:, 0:T_OWN],
                                            op0=OP.mult, op1=OP.subtract))
            mark["var2"] = st_v.n

        @block.scalar
        def _(s):
            s.wait_ge(st_v.h, mark["var2"])
            s.activation(out=t_rstd[:, 0:T_OWN], in_=t_rvar[:, 0:T_OWN], func=AF.Ln,
                         bias=t_cval[0:1, CV_EPS:CV_EPS + 1], scale=1.0)
            st_s.inc(s.activation(out=t_rstd[:, 0:T_OWN], in_=t_rstd[:, 0:T_OWN],
                                  func=AF.Exp, bias=0.0,
                                  scale=t_cval[0:1, CV_NHALF:CV_NHALF + 1]))
            mark["rstd2"] = st_s.n

        @block.vector
        def _(v):
            v.wait_ge(st_s.h, mark["rstd2"])
            st_v.inc(v.tensor_tensor(t_sdgb[:, :], t_sd[:, :], t_modt[:, 80:96], OP.mult))
            mark["b2"] = st_v.n
        es_o.close()

        es_b2 = ExitStack()
        ps_b2 = [es_b2.enter_context(nc.psum_tensor(f"ps_b2_{i}", [128, 512], F32))
                 for i in range(2)]
        b2_t_base, b2_v_base = st_t.n, st_v.n

        @block.tensor
        def _(t):
            t.wait_ge(st_v.h, mark["b2"])
            for j, row in enumerate([t_rstd, t_rmu]):
                for i in range(2):
                    idx = j * 2 + i
                    if idx >= 2:
                        t.wait_ge(st_v.h, b2_v_base + idx - 1)
                    st_t.inc(t.matmul(ps_b2[idx % 2][:, :], t_onesf[:, :],
                                      row[:, i * 512:(i + 1) * 512],
                                      start=True, stop=True))

        @block.vector
        def _(v):
            for i in range(2):
                v.wait_ge(st_t.h, b2_t_base + i + 1)
                st_v.inc(v.tensor_copy(t_rstd2b[:, i * 512:(i + 1) * 512], ps_b2[i % 2][:, :]))
            for i in range(2):
                idx = 2 + i
                v.wait_ge(st_t.h, b2_t_base + idx + 1)
                st_v.inc(v.scalar_tensor_tensor(out=t_b2b[:, i * 512:(i + 1) * 512],
                                                in0=ps_b2[idx % 2][:, :], scalar=-1.0,
                                                in1=t_rstd2b[:, i * 512:(i + 1) * 512],
                                                op0=OP.mult, op1=OP.mult))
            mark["bcast2"] = st_v.n
        es_b2.close()

        h2_v_base = st_v.n
        @block.sync
        def _(y):
            for c in range(KC):
                if c >= 2:
                    y.wait_ge(st_v.h, h2_v_base + c - 1)
                elif c == 1:
                    y.wait_ge(st_v.h, res15_v)
                WT(y, mark[f"x2sp{c}"])
                mark[f"x2re{c}"] = dx2re.issue(y.dma_start(t_x2re[c % 2][:, :], d_x2[c]))

        @block.vector
        def _(v):
            for c in range(KC):
                WT(v, mark[f"x2re{c}"])
                v.tensor_tensor(t_h2scr[:, :], t_x2re[c % 2][:, :], t_rstd2b[:, :], OP.mult)
                v.tensor_tensor(t_h2scr[:, :], t_h2scr[:, :], t_b2b[:, :], OP.add)
                inst = v.tensor_scalar(out=h2ap(c, 0, T_OWN), in0=t_h2scr[:, :],
                                       scalar1=t_modt[:, 64 + c:65 + c],
                                       scalar2=t_modt[:, 48 + c:49 + c],
                                       op0=OP.mult, op1=OP.add)
                st_v.inc(inst)
                mark[f"h2_{c}"] = st_v.n

        if PHASES < 12:
            emit_tail()
            return nc

        @block.gpsimd
        def _(g):
            for m in range(HC):
                if m >= 2:
                    g.wait_ge(f_wblk.h, 3 * KC + m - 1)
                else:
                    g.wait_ge(st_t.h, mark["stats2_mm"])
                    WT(g, mark["x2sp14"])
                    WT(g, mark["x2sp15"])
                mark[f"wu{m}"] = dblk.issue(g.dma_start(
                    t_wubuf[m % 2][:, :], d_wu[m], max_dma_last_dim=2048))
            if PHASES < 13:
                return
            for m in range(HC):
                g.wait_ge(f_wblk.h, 3 * KC + HC + m - 1)
                mark[f"wg{m}"] = dblk.issue(g.dma_start(
                    t_wubuf[m % 2][:, :], d_wg[m], max_dma_last_dim=2048))
            if PHASES < 14:
                return
            for m in range(KC):
                if m >= 2:
                    g.wait_ge(f_wd.h, m - 1)
                mark[f"wd{m}"] = dwd.issue(g.dma_start(
                    t_wdbuf[m % 2][:, :], d_wd[m], max_dma_last_dim=2048))

        def emit_ffn_ug(name, wprefix, is_gate, bank_wait_mark):
            es_u = ExitStack()
            ps_u = [[es_u.enter_context(nc.psum_tensor(f"ps_{name}{p}{i}", [128, 512], F32))
                     for i in range(2)] for p in range(2)]
            t_base, s_base, v_base = st_t.n, st_s.n, st_v.n
            fbase = f_wblk.n
            vper = 2 if is_gate else 1

            @block.vector
            def _(v):
                for m in range(HC):
                    WT(v, mark[f"{wprefix}{m}"])
                    st_v.inc(v.tensor_scalar(out=t_wubuf[m % 2][:, :], in0=t_wubuf[m % 2][:, :],
                                             scalar1=-128.0, scalar2=None, op0=OP.add))
                    if is_gate:
                        if m >= 1:
                            v.wait_ge(st_s.h, s_base + m)
                            last = None
                            for i in range(2):
                                last = v.tensor_tensor(upap(m - 1, i * 512, 512),
                                                       t_gtmp[(m - 1) % 2][:, i * 512:(i + 1) * 512],
                                                       upap(m - 1, i * 512, 512), OP.mult)
                            st_v.inc(last)
                        else:
                            st_v.inc(v.engine_nop())
                    assert st_v.n == v_base + vper * m + vper
                if is_gate:
                    v.wait_ge(st_s.h, s_base + HC)
                    last = None
                    for i in range(2):
                        last = v.tensor_tensor(upap(HC - 1, i * 512, 512),
                                               t_gtmp[(HC - 1) % 2][:, i * 512:(i + 1) * 512],
                                               upap(HC - 1, i * 512, 512), OP.mult)
                    st_v.inc(last)
                mark[f"{name}_vdone"] = st_v.n

            @block.tensor
            def _(t):
                if bank_wait_mark is not None:
                    t.wait_ge(st_v.h, mark[bank_wait_mark])
                if is_gate:
                    t.wait_ge(st_s.h, mark["up_done"])
                for m in range(HC):
                    t.wait_ge(st_v.h, v_base + vper * m + 1)
                    if m >= 2:
                        t.wait_ge(st_s.h, s_base + m - 1)
                    p = m % 2
                    last = None
                    for k in range(KC):
                        if m == 0:
                            t.wait_ge(st_v.h, mark[f"h2_{k}"])
                        for i in range(2):
                            last = t.matmul(ps_u[p][i][:, :],
                                            t_wubuf[m % 2][:, k * 128:(k + 1) * 128],
                                            h2ap(k, i * 512, 512),
                                            start=(k == 0), stop=(k == KC - 1))
                    f_wblk.inc(last)

            @block.scalar
            def _(s):
                for m in range(HC):
                    s.wait_ge(f_wblk.h, fbase + m + 1)
                    p = m % 2
                    last = None
                    for i in range(2):
                        if is_gate:
                            last = s.activation(out=t_gtmp[m % 2][:, i * 512:(i + 1) * 512],
                                                in_=ps_u[p][i][:, :], func=AF.Silu, bias=0.0,
                                                scale=t_sg[:, m:m + 1])
                        else:
                            last = s.activation(out=upap(m, i * 512, 512), in_=ps_u[p][i][:, :],
                                                func=AF.Copy, bias=0.0, scale=t_su[:, m:m + 1])
                    st_s.inc(last)
                    assert st_s.n == s_base + m + 1
            mark[f"{name}_done"] = st_s.n
            es_u.close()

        emit_ffn_ug("up", "wu", False, "bcast2")
        if PHASES < 13:
            emit_tail()
            return nc
        emit_ffn_ug("gate", "wg", True, None)
        if PHASES < 14:
            emit_tail()
            return nc

        es_d = ExitStack()
        ps_d = [[es_d.enter_context(nc.psum_tensor(f"ps_d{p}{i}", [128, 512], F32))
                 for i in range(2)] for p in range(2)]
        dn_t_base, dn_v_base, dn_s_base = st_t.n, st_v.n, st_s.n

        @block.sync
        def _(y):
            for m in range(KC):
                if m >= 2:
                    y.wait_ge(st_v.h, dn_v_base + 2 * (m - 1) + 2)
                else:
                    y.wait_ge(st_v.h, mark[f"h2_{14 + m}"])
                mark[f"x2rf{m}"] = dx2re.issue(y.dma_start(t_x2re[m % 2][:, :], d_x2[m]))
                if m >= 2:
                    y.wait_ge(st_v.h, dn_v_base + 2 * (m - 2) + 2)
                    mark[f"yout{m - 2}"] = dout.issue(
                        y.dma_start(d_y[m - 2], t_outbuf[(m - 2) % 2][:, :]))
            for m in (KC - 2, KC - 1):
                y.wait_ge(st_v.h, dn_v_base + 2 * m + 2)
                mark[f"yout{m}"] = dout.issue(y.dma_start(d_y[m], t_outbuf[m % 2][:, :]))
            y.wait_ge(dout.sems[0].h, dout.sems[0].n)
            y.wait_ge(dout.sems[1].h, dout.sems[1].n)

        @block.vector
        def _(v):
            v.wait_ge(st_s.h, mark["gate_done"])
            v.wait_ge(st_v.h, mark["gate_vdone"])
            for m in range(KC):
                WT(v, mark[f"wd{m}"])
                st_v.inc(v.tensor_scalar(out=t_wdbuf[m % 2][:, :], in0=t_wdbuf[m % 2][:, :],
                                         scalar1=-128.0, scalar2=None, op0=OP.add))
                v.wait_ge(st_s.h, dn_s_base + m + 1)
                WT(v, mark[f"x2rf{m}"])
                if m >= 2:
                    WT(v, mark[f"yout{m - 2}"])
                inst = v.tensor_tensor(t_outbuf[m % 2][:, :], t_dscr[:, :],
                                       t_x2re[m % 2][:, :], OP.add)
                st_v.inc(inst)
                assert st_v.n == dn_v_base + 2 * m + 2

        @block.tensor
        def _(t):
            for m in range(KC):
                t.wait_ge(st_v.h, dn_v_base + 2 * m + 1)
                if m >= 2:
                    t.wait_ge(st_s.h, dn_s_base + m - 1)
                p = m % 2
                last = None
                for k in range(HC):
                    for i in range(2):
                        last = t.matmul(ps_d[p][i][:, :],
                                        t_wdbuf[m % 2][:, k * 128:(k + 1) * 128],
                                        upap(k, i * 512, 512),
                                        start=(k == 0), stop=(k == HC - 1))
                f_wd.inc(last)

        @block.scalar
        def _(s):
            for m in range(KC):
                s.wait_ge(f_wd.h, m + 1)
                if m >= 1:
                    s.wait_ge(st_v.h, dn_v_base + 2 * (m - 1) + 2)
                p = m % 2
                last = None
                for i in range(2):
                    last = s.activation(out=t_dscr[:, i * 512:(i + 1) * 512],
                                        in_=ps_d[p][i][:, :], func=AF.Copy, bias=0.0,
                                        scale=t_sdgb[:, m:m + 1])
                st_s.inc(last)
                assert st_s.n == dn_s_base + m + 1


    return nc


def _stage_weights(Wq, Wk, Wv, Wo, wg, wu, wd, wmod):
    perm = np.concatenate([np.arange(0, 128, 2), np.arange(1, 128, 2)])
    perm_full = np.concatenate([h * 128 + perm for h in range(NH)])

    def blkT(Wp, mc, kc):
        return np.ascontiguousarray(
            Wp.reshape(mc, 128, kc, 128).transpose(0, 3, 2, 1)).reshape(mc, 128, kc * 128)

    wq_m = blkT(np.ascontiguousarray(Wq[perm_full]), KC, KC)
    wk_m = blkT(np.ascontiguousarray(Wk[perm_full]), KC, KC)
    wo_m = blkT(np.ascontiguousarray(Wo), KC, KC)
    wv_k = np.ascontiguousarray(Wv.T).reshape(KC, 128, D)
    def lo8(a):
        return np.ascontiguousarray(a.view(np.uint8)[..., 0::4])

    wg_m = lo8(blkT(np.ascontiguousarray(wg), HC, KC))
    wu_m = lo8(blkT(np.ascontiguousarray(wu), HC, KC))
    wd_m = lo8(blkT(np.ascontiguousarray(wd), KC, HC))
    wmod_k = np.ascontiguousarray(wmod).reshape(KC, 128, 6 * D)
    return wq_m, wk_m, wv_k, wo_m, wg_m, wu_m, wd_m, wmod_k


def _rope_tables(g):
    inv = 10000.0 ** (-(np.arange(64, dtype=np.float64) / 64.0))
    pos = g * 1024 - 256 + np.arange(T_WIN, dtype=np.float64)
    fr = np.outer(inv, pos)
    cos = np.cos(fr)
    sin = np.sin(fr)
    cs1 = np.concatenate([cos, cos], axis=0).astype(np.float32)
    cs2 = np.concatenate([-sin, sin], axis=0).astype(np.float32)
    return np.ascontiguousarray(cs1), np.ascontiguousarray(cs2)


def _masks(g):
    out = np.zeros((2, 128, 2, 512), np.float32)
    v = np.arange(512)
    for cls in range(2):
        for wt in range(2):
            for p2 in range(128):
                wrow = wt * 128 + p2
                lo = 256 if (cls == 0 and g == 0) else 0
                bad = (v < lo) | (v >= 257 + wrow)
                out[cls, p2, wt, bad] = NEG
    return np.ascontiguousarray(out.reshape(2, 128, 1024))


def kernel(L0_attn_x, L0_attn_Wq, L0_attn_Wk, L0_attn_Wv, L0_attn_Wo,
           L0_ffn_wq_gate, L0_ffn_s_gate, L0_ffn_wq_up, L0_ffn_s_up,
           L0_ffn_wq_down, L0_ffn_s_down, L0_c, L0_wmod):
    x = np.asarray(L0_attn_x, np.float32)
    wq_m, wk_m, wv_k, wo_m, wg_m, wu_m, wd_m, wmod_k = _stage_weights(
        np.asarray(L0_attn_Wq, np.float32), np.asarray(L0_attn_Wk, np.float32),
        np.asarray(L0_attn_Wv, np.float32), np.asarray(L0_attn_Wo, np.float32),
        np.asarray(L0_ffn_wq_gate, np.int32), np.asarray(L0_ffn_wq_up, np.int32),
        np.asarray(L0_ffn_wq_down, np.int32), np.asarray(L0_wmod, np.float32))

    s_gate_t = np.ascontiguousarray(np.asarray(L0_ffn_s_gate, np.float32).reshape(HC, 128).T)
    s_up_t = np.ascontiguousarray(np.asarray(L0_ffn_s_up, np.float32).reshape(HC, 128).T)
    s_down_t = np.ascontiguousarray(np.asarray(L0_ffn_s_down, np.float32).reshape(KC, 128).T)

    id_bf = np.eye(128, dtype=ml_dtypes.bfloat16)
    id_f = np.eye(128, dtype=np.float32)
    swap = np.zeros((128, 128), ml_dtypes.bfloat16)
    swap[(np.arange(128) + 64) % 128, np.arange(128)] = 1
    ones_b = np.ones((128, 1), ml_dtypes.bfloat16)
    ones_f = np.ones((1, 128), np.float32)
    cvals = np.zeros((128, 8), np.float32)
    cvals[:, CV_N128] = -128.0
    cvals[:, CV_EPS] = 1e-5
    cvals[:, CV_NHALF] = -0.5

    xT = np.ascontiguousarray(x.transpose(0, 2, 1))

    in_maps = []
    for core in range(8):
        b, g = divmod(core, 2)
        if g == 0:
            xc = np.concatenate([xT[b][:, 0:256], xT[b][:, 0:1024]], axis=1)
        else:
            xc = xT[b][:, 768:2048]
        x_in = np.ascontiguousarray(xc).reshape(KC, 128, T_WIN)
        c_col = np.asarray(L0_c, np.float32)[b].reshape(KC, 128).T
        c_rep = np.ascontiguousarray(np.repeat(c_col[:, :, None], 4, axis=2).reshape(128, 64))
        cs1, cs2 = _rope_tables(g)
        msk = _masks(g)
        in_maps.append(dict(
            x_in=x_in, wq_m=wq_m, wk_m=wk_m, wv_k=wv_k, wo_m=wo_m,
            wg_m=wg_m, wu_m=wu_m, wd_m=wd_m, wmod_k=wmod_k, c_rep=c_rep,
            s_gate_t=s_gate_t, s_up_t=s_up_t, s_down_t=s_down_t,
            cs1=cs1, cs2=cs2, masks=msk, id_bf=id_bf, id_f=id_f,
            swap_bf=swap, ones_b=ones_b, ones_f=ones_f, cvals=cvals,
        ))

    nc = build_nc()
    res = run_bass_kernel_spmd(nc, in_maps, core_ids=list(range(8)))

    out = np.empty((4, D, D), np.float32)
    for core in range(8):
        b, g = divmod(core, 2)
        yc = np.asarray(res.results[core]["y"]).reshape(D, T_OWN)
        out[b][:, g * 1024:(g + 1) * 1024] = yc
    return np.ascontiguousarray(out.transpose(0, 2, 1))


# version 15
# speedup vs baseline: 1.1846x; 1.1846x over previous
import os
import numpy as np
import ml_dtypes
import concourse.bass as bass
import concourse.mybir as mybir
from concourse.bass_utils import run_bass_kernel_spmd
from contextlib import ExitStack

F32 = mybir.dt.float32
BF16 = mybir.dt.bfloat16
I32 = mybir.dt.int32
U8 = mybir.dt.uint8
OP = mybir.AluOpType
AF = mybir.ActivationFunctionType
AX = mybir.AxisListType

D = 2048
KC = 16
HID = 5632
HC = 44
NH = 16
W = 256
T_OWN = 1024
T_WIN = 1280
SCALE = float(128) ** -0.5
NEG = -1.0e30

O_MISC = 0
O_T1 = 3072
O_A = 13312
O_B = 95232
O_C = 136192

O_CS1 = O_T1
O_CS2 = O_T1 + 5120
O_RSTD2B = O_T1
O_B2B = O_T1 + 4096

O_KROT = O_A
O_QROT = O_A + 40960
O_WKBUF = O_A + 73728
O_UPLO = O_A

O_H = O_B
O_ATTN = O_B
O_UPHI = O_B
O_H2 = O_B + 8192

O_WMODBUF = O_C
O_MODSB = O_C + 16384
O_XB = O_C + 32768
O_XSQ = O_C + 37888
O_RSTDB = O_C + 43008
O_BB = O_C + 48128
O_SROWS = O_C + 53248
O_HTMP = O_C + 58368
O_RSA = O_C + 63488
O_VTM = O_C
O_WVBUF = O_C + 40960
O_MSK = O_C + 40960
O_EXPN = O_C + 57344
O_ATSB = O_C + 65536
O_MASKS = 3072
O_X2B = O_C
O_X2SQ = O_C + 2048
O_X2BUF = O_C + 4096
O_XREBUF = O_C + 12288
O_X2RE0 = O_C + 20480
O_X2RE1 = O_C + 57344
O_H2SCR = O_C + 12288
O_DSCR = O_C + 16384
O_WUBUF = O_C
O_WDBUF = O_C + 24576
O_OUTBUF = O_C + 47104
O_GTMP = O_C + 63488
assert O_GTMP + 4096 <= O_C + 69808
CAP = 206000

O_IDB = 0
O_IDF = 256
O_SWAP = 768
O_ONESB = 1024
O_ONESF = 1056
O_CVAL = 1568
O_CREP = 1600
O_SILC = 1856
O_MODT = 1984
O_SG = 2368
O_SU = 2560
O_SD = 2752
O_SDGB = 2816
O_TINY = 2880
assert O_TINY + 64 <= O_T1

CV_N128, CV_EPS, CV_NHALF = 0, 1, 2
PHASES = int(os.environ.get("MK_PHASES", "99"))


def ts_slices(total, step=512):
    out, o = [], 0
    while o < total:
        out.append((o, min(step, total - o)))
        o += step
    return out


class Sem:
    def __init__(self, nc, es, name):
        self.h = es.enter_context(nc.semaphore(name))
        self.n = 0

    def inc(self, inst, k=1):
        inst.then_inc(self.h, k)
        self.n += k
        return self.n


class DStream:

    def __init__(self, nc, es, name):
        self.sems = [Sem(nc, es, name + "0"), Sem(nc, es, name + "1")]
        self.i = 0

    def issue(self, inst):
        s = self.sems[self.i % 2]
        s.inc(inst, 16)
        self.i += 1
        return (s.h, s.n)


def build_nc():
    nc = bass.Bass("TRN2", target_bir_lowering=False, detect_race_conditions=False)

    d_x = nc.dram_tensor("x_in", [KC, 128, T_WIN], F32, kind="ExternalInput").ap()
    d_wq = nc.dram_tensor("wq_m", [KC, 128, D], F32, kind="ExternalInput").ap()
    d_wk = nc.dram_tensor("wk_m", [KC, 128, D], F32, kind="ExternalInput").ap()
    d_wv = nc.dram_tensor("wv_k", [KC, 128, D], F32, kind="ExternalInput").ap()
    d_wo = nc.dram_tensor("wo_m", [KC, 128, D], F32, kind="ExternalInput").ap()
    d_wg = nc.dram_tensor("wg_m", [HC, 128, D], U8, kind="ExternalInput").ap()
    d_wu = nc.dram_tensor("wu_m", [HC, 128, D], U8, kind="ExternalInput").ap()
    d_wd = nc.dram_tensor("wd_m", [KC, 128, HID], U8, kind="ExternalInput").ap()
    d_wmod = nc.dram_tensor("wmod_k", [KC, 128, 6 * D], F32, kind="ExternalInput").ap()
    d_crep = nc.dram_tensor("c_rep", [128, 64], F32, kind="ExternalInput").ap()
    d_sg = nc.dram_tensor("s_gate_t", [128, HC], F32, kind="ExternalInput").ap()
    d_su = nc.dram_tensor("s_up_t", [128, HC], F32, kind="ExternalInput").ap()
    d_sd = nc.dram_tensor("s_down_t", [128, KC], F32, kind="ExternalInput").ap()
    d_cs1 = nc.dram_tensor("cs1", [128, T_WIN], F32, kind="ExternalInput").ap()
    d_cs2 = nc.dram_tensor("cs2", [128, T_WIN], F32, kind="ExternalInput").ap()
    d_masks = nc.dram_tensor("masks", [2, 128, 1024], F32, kind="ExternalInput").ap()
    d_idb = nc.dram_tensor("id_bf", [128, 128], BF16, kind="ExternalInput").ap()
    d_idf = nc.dram_tensor("id_f", [128, 128], F32, kind="ExternalInput").ap()
    d_swap = nc.dram_tensor("swap_bf", [128, 128], BF16, kind="ExternalInput").ap()
    d_onesb = nc.dram_tensor("ones_b", [128, 1], BF16, kind="ExternalInput").ap()
    d_onesf = nc.dram_tensor("ones_f", [1, 128], F32, kind="ExternalInput").ap()
    d_cval = nc.dram_tensor("cvals", [128, 8], F32, kind="ExternalInput").ap()

    d_y = nc.dram_tensor("y", [KC, 128, T_OWN], F32, kind="ExternalOutput").ap()
    d_x2 = nc.dram_tensor("x2_scratch", [KC, 128, T_OWN], F32, kind="Internal").ap()

    _arena = nc.alloc_sbuf_tensor("manual_arena", [128, CAP], mybir.dt.uint8)
    SBUF_BASE = nc.lookup_mloc(_arena).addr

    def AT(name, shape, dtype, off):
        return nc.alloc_sbuf_tensor_at(name, shape, dtype, offset=SBUF_BASE + off).ap()

    t_x = AT("x", [128, KC * T_WIN], F32, O_A)
    t_h = AT("h", [128, KC * T_WIN], BF16, O_H)
    t_krot = AT("krot", [128, NH * T_WIN], BF16, O_KROT)
    t_qrot = AT("qrot", [128, NH * T_OWN], BF16, O_QROT)
    t_vtm = AT("vtm", [128, 10 * D], BF16, O_VTM)
    t_attn = AT("attn_fm", [128, NH * T_OWN], BF16, O_ATTN)
    t_uplo = AT("up_lo", [128, 40 * T_OWN], BF16, O_UPLO)
    t_uphi = AT("up_hi", [128, 4 * T_OWN], BF16, O_UPHI)
    t_h2 = AT("h2", [128, KC * T_OWN], BF16, O_H2)

    t_wmodbuf = [AT(f"wmodbuf{i}", [128, 4096], BF16, O_WMODBUF + i * 8192) for i in range(2)]
    t_wmq = [AT(f"wmq{i}", [128, 4096], BF16, O_QROT + i * 8192) for i in range(4)]
    t_wma = [AT(f"wma{i}", [128, 4096], BF16, O_MODSB + i * 8192) for i in range(2)]
    t_wmc = [AT(f"wmc{i}", [128, 4096], BF16, O_WVBUF + i * 8192) for i in range(2)]
    t_modsb = AT("mod_sb", [4, 4096], F32, O_MODSB)
    t_xb = [AT(f"xb{i}", [128, T_WIN], BF16, O_XB + i * 2560) for i in range(2)]
    t_xsq = [AT(f"xsq{i}", [128, T_WIN], BF16, O_XSQ + i * 2560) for i in range(2)]
    t_rstdb = AT("rstd_b", [128, T_WIN], F32, O_RSTDB)
    t_bb = AT("b_b", [128, T_WIN], F32, O_BB)
    t_rmu = AT("r_mu", [1, T_WIN], F32, O_SROWS)
    t_rvar = AT("r_var", [1, T_WIN], F32, O_XSQ)
    t_rstd = AT("r_std", [1, T_WIN], F32, O_XB)
    t_htmp = AT("htmp", [128, T_WIN], F32, O_HTMP)
    t_wkbuf = [AT(f"wkbuf{i}", [128, D], BF16, O_WKBUF + i * 4096) for i in range(2)]
    t_wubuf = [AT(f"wubuf{i}", [128, D], BF16, O_WUBUF + i * 4096) for i in range(2)]
    t_rsa = AT("rope_a", [128, 512], F32, O_RSA)
    t_wvbuf = AT("wvbuf", [128, KC * 512], BF16, O_WVBUF)
    t_wvbuf2a = AT("wvbuf2a", [128, 12 * 512], BF16, O_C + 57344)
    t_gscr = AT("gscr", [4, 1024], F32, O_X2BUF)
    t_masks = AT("masks_sb", [128, 2 * 1024], F32, O_MASKS)
    t_msk2 = [AT(f"msk2_{p}", [128, 2048], F32, O_MSK + p * 8192) for p in range(2)]
    t_expn2 = [AT(f"expn2_{p}", [128, 2048], BF16, O_EXPN + p * 4096) for p in range(2)]
    t_atsb2 = [AT(f"atsb2_{h}", [128, 1024], BF16, O_ATSB + h * 2048) for h in range(2)]
    t_x2b = AT("x2b", [128, T_OWN], BF16, O_X2B)
    t_x2sq = AT("x2sq", [128, T_OWN], BF16, O_X2SQ)
    t_x2buf = [AT(f"x2buf{i}", [128, T_OWN], F32, O_X2BUF + i * 4096) for i in range(2)]
    t_xrebuf = [AT(f"xrebuf{i}", [128, T_OWN], F32, O_XREBUF + i * 4096) for i in range(2)]
    t_x2re = [AT("x2re0", [128, T_OWN], F32, O_X2RE0),
              AT("x2re1", [128, T_OWN], F32, O_X2RE1)]
    t_h2scr = AT("h2scr", [128, T_OWN], F32, O_H2SCR)
    t_dscr = AT("dscr", [128, T_OWN], F32, O_DSCR)
    t_wdbuf = [AT(f"wdbuf{i}", [128, HID], BF16, O_WDBUF + i * 11264) for i in range(2)]
    t_outbuf = [AT(f"outbuf{i}", [128, T_OWN], F32, O_OUTBUF + i * 4096) for i in range(2)]
    t_gtmp = [AT(f"gtmp{i}", [128, T_OWN], BF16, O_GTMP + i * 2048) for i in range(2)]

    t_cs1 = AT("cs1_sb", [128, T_WIN], F32, O_CS1)
    t_cs2 = AT("cs2_sb", [128, T_WIN], F32, O_CS2)
    t_rstd2b = AT("rstd2_b", [128, T_OWN], F32, O_RSTD2B)
    t_b2b = AT("b2_b", [128, T_OWN], F32, O_B2B)

    t_idb = AT("id_bf_sb", [128, 128], BF16, O_IDB)
    t_idf = AT("id_f_sb", [128, 128], F32, O_IDF)
    t_swap = AT("swap_sb", [128, 128], BF16, O_SWAP)
    t_onesb = AT("ones_b_sb", [128, 1], BF16, O_ONESB)
    t_onesf = AT("ones_f_sb", [1, 128], F32, O_ONESF)
    t_cval = AT("cvals_sb", [128, 8], F32, O_CVAL)
    t_crep = AT("c_rep_sb", [128, 64], F32, O_CREP)
    t_silc = AT("silu_c", [128, 64], BF16, O_SILC)
    t_modt = AT("modT", [128, 96], F32, O_MODT)
    t_sg = AT("sg_sb", [128, HC], F32, O_SG)
    t_su = AT("su_sb", [128, HC], F32, O_SU)
    t_sd = AT("sd_sb", [128, KC], F32, O_SD)
    t_sdgb = AT("sdgb", [128, KC], F32, O_SDGB)
    t_tiny = AT("tiny", [128, 24], F32, O_TINY)

    es = ExitStack()
    sy = Sem(nc, es, "sy")
    sg = Sem(nc, es, "sgd")
    st_t = Sem(nc, es, "st_t")
    st_v = Sem(nc, es, "st_v")
    st_s = Sem(nc, es, "st_s")
    s_out = Sem(nc, es, "s_out")
    s_ev = Sem(nc, es, "s_attev")
    f_wmod = Sem(nc, es, "f_wmod")
    f_wblk = Sem(nc, es, "f_wblk")
    f_wv = Sem(nc, es, "f_wv")
    f_wd = Sem(nc, es, "f_wd")
    f_xre = Sem(nc, es, "f_xre")
    f_x2re = Sem(nc, es, "f_x2re")

    dmod = DStream(nc, es, "dmod")
    dblk = DStream(nc, es, "dblk")
    dwd = DStream(nc, es, "dwd")
    dxre = DStream(nc, es, "dxre")
    dsp = DStream(nc, es, "dsp")
    dx2re = DStream(nc, es, "dx2re")
    dout = DStream(nc, es, "dout")
    s_wv = Sem(nc, es, "s_wv")
    s_msk = Sem(nc, es, "s_msk")

    mark = {}

    def WT(eng, mk):
        eng.wait_ge(mk[0], mk[1])

    def hap(c, o, n):
        return t_h[:, c * T_WIN + o: c * T_WIN + o + n]

    def h2ap(c, o, n):
        return t_h2[:, c * T_OWN + o: c * T_OWN + o + n]

    def upap(m, o, n):
        if m < 40:
            return t_uplo[:, m * T_OWN + o: m * T_OWN + o + n]
        return t_uphi[:, (m - 40) * T_OWN + o: (m - 40) * T_OWN + o + n]

    def wmbuf(pi, k):
        if pi == 1 and 2 <= k < 6:
            return t_wmq[k - 2]
        if pi == 0 and 2 <= k < 4:
            return t_wma[k - 2]
        if pi == 2 and 2 <= k < 4:
            return t_wmc[k - 2]
        return t_wmodbuf[k % 2]

    def wvsl(np_, k):
        if np_ % 2 and k < 12:
            return t_wvbuf2a[:, k * 512:(k + 1) * 512]
        return t_wvbuf[:, k * 512:(k + 1) * 512]

    with es, nc.Block() as block:
        t_dumpA = AT("dumpA", [128, 16 * 1024], F32, O_A)
        t_dumpB = AT("dumpB", [128, 10 * 1024], F32, O_B)
        t_dumpC = AT("dumpC", [128, 6 * 1024], F32, O_C)

        def emit_tail():
            dump_mode = os.environ.get("MK_DUMP", "A")
            simsafe = os.environ.get("MK_SIM", "") == "1"

            @block.sync
            def _(y):
                y.wait_ge(st_t.h, st_t.n)
                y.wait_ge(st_v.h, st_v.n)
                y.wait_ge(st_s.h, st_s.n)
                y.wait_ge(s_ev.h, s_ev.n)
                for m in range(KC):
                    if simsafe:
                        srcap = t_cs1[:, 0:1024] if PHASES < 9 else t_masks[:, 0:1024]
                    elif dump_mode == "A":
                        srcap = t_dumpA[:, m * 1024:(m + 1) * 1024]
                    else:
                        srcap = (t_dumpB[:, m * 1024:(m + 1) * 1024] if m < 10
                                 else t_dumpC[:, (m - 10) * 1024:(m - 9) * 1024])
                    mark[f"yout{m}"] = dout.issue(y.dma_start(d_y[m], srcap))
                y.wait_ge(dout.sems[0].h, dout.sems[0].n)
                y.wait_ge(dout.sems[1].h, dout.sems[1].n)

        @block.sync
        def _(y):
            for c in range(KC):
                sy.inc(y.dma_start(t_x[:, c * T_WIN:(c + 1) * T_WIN], d_x[c]), 16)
            for dst, src in [(t_idb, d_idb), (t_idf, d_idf), (t_swap, d_swap),
                             (t_onesb, d_onesb), (t_onesf, d_onesf), (t_cval, d_cval),
                             (t_crep, d_crep), (t_sg, d_sg), (t_su, d_su), (t_sd, d_sd),
                             (t_cs1, d_cs1), (t_cs2, d_cs2)]:
                sy.inc(y.dma_start(dst[:, :], src), 16)
            mark["consts"] = sy.n

        @block.gpsimd
        def _(g):
            if PHASES < 2:
                return
            for k in range(KC):
                if k in (4, 5):
                    g.wait_ge(f_wmod.h, k - 3)
                elif k >= 6:
                    g.wait_ge(f_wmod.h, k - 1)
                mark[f"wmodA{k}"] = dmod.issue(g.dma_start(
                    wmbuf(0, k)[:, :], d_wmod[k][:, 0:4096], max_dma_last_dim=2048))

        stats_t_base = st_t.n

        @block.scalar
        def _(s):
            s.wait_ge(sy.h, mark["consts"])
            st_s.inc(s.activation(out=t_silc[:, :], in_=t_crep[:, :], func=AF.Silu))
            mark["silc"] = st_s.n
            s.wait_ge(sy.h, mark["consts"])
            for c in range(KC):
                if c >= 2:
                    s.wait_ge(st_t.h, stats_t_base + c - 1)
                st_s.inc(s.activation(out=t_xsq[c % 2][:, :],
                                      in_=t_x[:, c * T_WIN:(c + 1) * T_WIN], func=AF.Square))
                mark[f"xsq{c}"] = st_s.n

        @block.vector
        def _(v):
            v.wait_ge(sy.h, mark["consts"])
            for c in range(KC):
                if c >= 2:
                    v.wait_ge(st_t.h, stats_t_base + c - 1)
                st_v.inc(v.tensor_copy(t_xb[c % 2][:, :], t_x[:, c * T_WIN:(c + 1) * T_WIN]))
                mark[f"xb{c}"] = st_v.n

        es_p = ExitStack()
        ps_sum = [es_p.enter_context(nc.psum_tensor(f"ps_sum{i}", [1, 512], F32)) for i in range(3)]
        ps_sq = [es_p.enter_context(nc.psum_tensor(f"ps_sq{i}", [1, 512], F32)) for i in range(3)]

        @block.tensor
        def _(t):
            t.wait_ge(sy.h, mark["consts"])
            for c in range(KC):
                t.wait_ge(st_v.h, mark[f"xb{c}"])
                t.wait_ge(st_s.h, mark[f"xsq{c}"])
                last = None
                for i, (o, n) in enumerate(ts_slices(T_WIN)):
                    t.matmul(ps_sum[i][:, 0:n], t_onesb[:, :], t_xb[c % 2][:, o:o + n],
                             start=(c == 0), stop=(c == KC - 1))
                    last = t.matmul(ps_sq[i][:, 0:n], t_onesb[:, :], t_xsq[c % 2][:, o:o + n],
                                    start=(c == 0), stop=(c == KC - 1))
                st_t.inc(last)
            assert st_t.n == stats_t_base + KC
            mark["stats_mm"] = st_t.n

        @block.vector
        def _(v):
            v.wait_ge(st_t.h, mark["stats_mm"])
            for i, (o, n) in enumerate(ts_slices(T_WIN)):
                v.tensor_copy(t_rmu[:, o:o + n], ps_sum[i][:, 0:n])
                v.tensor_copy(t_rvar[:, o:o + n], ps_sq[i][:, 0:n])
            v.tensor_scalar(out=t_rmu[:, :], in0=t_rmu[:, :],
                            scalar1=1.0 / 2048.0, scalar2=None, op0=OP.mult)
            v.tensor_tensor(t_rstd[:, :], t_rmu[:, :], t_rmu[:, :], OP.mult)
            st_v.inc(v.scalar_tensor_tensor(out=t_rvar[:, :], in0=t_rvar[:, :],
                                            scalar=1.0 / 2048.0, in1=t_rstd[:, :],
                                            op0=OP.mult, op1=OP.subtract))
            mark["var_row"] = st_v.n

        @block.scalar
        def _(s):
            s.wait_ge(st_v.h, mark["var_row"])
            s.activation(out=t_rstd[:, :], in_=t_rvar[:, :], func=AF.Ln,
                         bias=t_cval[0:1, CV_EPS:CV_EPS + 1], scale=1.0)
            st_s.inc(s.activation(out=t_rstd[:, :], in_=t_rstd[:, :], func=AF.Exp,
                                  bias=0.0, scale=t_cval[0:1, CV_NHALF:CV_NHALF + 1]))
            mark["rstd_row"] = st_s.n

        es_bc = ExitStack()
        ps_bc = [es_bc.enter_context(nc.psum_tensor(f"ps_bc{i}", [128, 512], F32)) for i in range(2)]
        bc_t_base = st_t.n
        bc_v_base = st_v.n

        @block.tensor
        def _(t):
            t.wait_ge(st_s.h, mark["rstd_row"])
            for j, row in enumerate([t_rstd, t_rmu]):
                for i, (o, n) in enumerate(ts_slices(T_WIN)):
                    idx = j * 3 + i
                    if idx >= 2:
                        t.wait_ge(st_v.h, bc_v_base + idx - 1)
                    st_t.inc(t.matmul(ps_bc[idx % 2][:, 0:n], t_onesf[:, :],
                                      row[:, o:o + n], start=True, stop=True))

        @block.vector
        def _(v):
            for i, (o, n) in enumerate(ts_slices(T_WIN)):
                v.wait_ge(st_t.h, bc_t_base + i + 1)
                st_v.inc(v.tensor_copy(t_rstdb[:, o:o + n], ps_bc[i % 2][:, 0:n]))
            for i, (o, n) in enumerate(ts_slices(T_WIN)):
                idx = 3 + i
                v.wait_ge(st_t.h, bc_t_base + idx + 1)
                st_v.inc(v.scalar_tensor_tensor(out=t_bb[:, o:o + n], in0=ps_bc[idx % 2][:, 0:n],
                                                scalar=-1.0, in1=t_rstdb[:, o:o + n],
                                                op0=OP.mult, op1=OP.mult))
            mark["bcast_done"] = st_v.n
        es_bc.close()
        es_p.close()

        if PHASES < 2:
            emit_tail()
            return nc

        def emit_mod_pass(pi, wmark_prefix, bank_wait_mark):
            es_m = ExitStack()
            ps_mod = [es_m.enter_context(nc.psum_tensor(f"ps_mod{pi}_{i}", [4, 512], F32))
                      for i in range(8)]

            @block.tensor
            def _(t):
                t.wait_ge(st_s.h, mark["silc"])
                t.wait_ge(st_v.h, mark[bank_wait_mark])
                for k in range(KC):
                    WT(t, mark[f"{wmark_prefix}{k}"])
                    last = None
                    for i in range(8):
                        last = t.matmul(ps_mod[i][:, :], t_silc[:, 4 * k:4 * k + 4],
                                        wmbuf(pi, k)[:, i * 512:(i + 1) * 512],
                                        start=(k == 0), stop=(k == KC - 1))
                    f_wmod.inc(last)
                mark[f"mod{pi}_mm"] = (f_wmod.h, f_wmod.n)

            @block.vector
            def _(v):
                WT(v, mark[f"mod{pi}_mm"])
                last = None
                for i in range(8):
                    last = v.tensor_copy(t_modsb[:, i * 512:(i + 1) * 512], ps_mod[i][:, :])
                if pi == 0:
                    last = v.tensor_scalar(out=t_modsb[:, 2048:4096], in0=t_modsb[:, 2048:4096],
                                           scalar1=1.0, scalar2=None, op0=OP.add)
                elif pi == 2:
                    last = v.tensor_scalar(out=t_modsb[:, 0:2048], in0=t_modsb[:, 0:2048],
                                           scalar1=1.0, scalar2=None, op0=OP.add)
                if pi in (1, 2):
                    base = 0 if pi == 1 else 2048
                    for hh in range(2):
                        gsl = t_modsb[:, base + hh * 1024: base + (hh + 1) * 1024]
                        scr = t_gscr[0:4, 0:1024] if pi == 2 else t_htmp[0:4, 0:1024]
                        v.tensor_scalar(out=scr, in0=gsl, scalar1=0.0, scalar2=None, op0=OP.is_lt)
                        v.tensor_scalar(out=scr, in0=scr, scalar1=-2.0, scalar2=1.0,
                                        op0=OP.mult, op1=OP.add)
                        v.tensor_tensor(gsl, gsl, scr, OP.mult)
                        v.tensor_scalar(out=gsl, in0=gsl, scalar1=0.25, scalar2=None,
                                        op0=OP.max)
                        last = v.tensor_tensor(gsl, gsl, scr, OP.mult)
                st_v.inc(last)
                mark[f"mod{pi}_adj"] = st_v.n

            es_m.close()
            es_t = ExitStack()
            ps_mt = es_t.enter_context(nc.psum_tensor(f"ps_mt{pi}", [128, 128], F32))

            @block.tensor
            def _(t):
                t.wait_ge(st_v.h, mark[f"mod{pi}_adj"])
                last = None
                for i in range(32):
                    last = t.transpose(ps_mt[:, 4 * i:4 * i + 4],
                                       t_modsb[:, i * 128:(i + 1) * 128], t_idf[0:4, 0:4])
                st_t.inc(last)
                mark[f"mod{pi}_tr"] = st_t.n

            @block.vector
            def _(v):
                v.wait_ge(st_t.h, mark[f"mod{pi}_tr"])
                src = ps_mt.ap().rearrange("p (i r) -> p i r", r=4)[:, :, 0:1]
                st_v.inc(v.tensor_copy(t_modt[:, 32 * pi:32 * pi + 32], src))
                mark[f"mod{pi}_done"] = st_v.n
            es_t.close()

        if PHASES >= 3:
            @block.vector
            def _(v):
                for c in range(KC):
                    xc = t_x[:, c * T_WIN:(c + 1) * T_WIN]
                    v.tensor_tensor(t_htmp[:, :], xc, t_rstdb[:, :], OP.mult)
                    st_v.inc(v.tensor_tensor(hap(c, 0, T_WIN), t_htmp[:, :], t_bb[:, :], OP.add))
                    mark[f"hpre{c}"] = st_v.n

        emit_mod_pass(0, "wmodA", "bcast_done")

        if PHASES < 3:
            emit_tail()
            return nc

        @block.vector
        def _(v):
            for c in range(KC):
                st_v.inc(v.tensor_scalar(out=hap(c, 0, T_WIN), in0=hap(c, 0, T_WIN),
                                         scalar1=t_modt[:, 16 + c:17 + c],
                                         scalar2=t_modt[:, c:c + 1],
                                         op0=OP.mult, op1=OP.add))
                mark[f"h{c}"] = st_v.n

        if PHASES < 4:
            emit_tail()
            return nc

        @block.gpsimd
        def _(g):
            for m in range(KC):
                if m >= 2:
                    g.wait_ge(f_wblk.h, m - 1)
                else:
                    g.wait_ge(st_v.h, mark[f"h{KC - 1}"])
                mark[f"wk{m}"] = dblk.issue(g.dma_start(
                    t_wkbuf[m % 2][:, :], d_wk[m], max_dma_last_dim=2048))
            if PHASES < 5:
                return
            for k in range(KC):
                if k < 2 or k >= 8:
                    g.wait_ge(f_wmod.h, KC + k - 1)
                elif k >= 6:
                    g.wait_ge(f_wmod.h, KC + k - 5)
                mark[f"wmodB{k}"] = dmod.issue(g.dma_start(
                    wmbuf(1, k)[:, :], d_wmod[k][:, 4096:8192], max_dma_last_dim=2048))
            if PHASES < 6:
                return
            for m in range(KC):
                g.wait_ge(f_wblk.h, KC + m - 1)
                mark[f"wq{m}"] = dblk.issue(g.dma_start(
                    t_wkbuf[m % 2][:, :], d_wq[m], max_dma_last_dim=2048))
            if PHASES < 7:
                return
            for k in range(KC):
                if k < 2:
                    g.wait_ge(f_wmod.h, 2 * KC + k - 1)
                elif k in (4, 5):
                    g.wait_ge(f_wmod.h, 2 * KC + k - 3)
                elif k >= 6:
                    g.wait_ge(f_wmod.h, 2 * KC + k - 1)
                mark[f"wmodC{k}"] = dmod.issue(g.dma_start(
                    wmbuf(2, k)[:, :], d_wmod[k][:, 8192:12288], max_dma_last_dim=2048))
            if PHASES < 8:
                return
            for np_ in range(4):
                if np_ == 0:
                    g.wait_ge(f_wmod.h, 2 * KC + 4)
                if np_ >= 2:
                    g.wait_ge(f_wv.h, np_ - 1)
                for k in range(12):
                    s_wv.inc(g.dma_start(wvsl(np_, k),
                                         d_wv[k][:, np_ * 512:(np_ + 1) * 512],
                                         max_dma_last_dim=2048), 16)
                mark[f"wv{np_}a"] = (s_wv.h, s_wv.n)
                if np_ >= 1:
                    g.wait_ge(f_wv.h, np_)
                for k in range(12, KC):
                    s_wv.inc(g.dma_start(wvsl(np_, k),
                                         d_wv[k][:, np_ * 512:(np_ + 1) * 512],
                                         max_dma_last_dim=2048), 16)
                mark[f"wv{np_}"] = (s_wv.h, s_wv.n)
            if PHASES < 10:
                return
            for m in range(KC):
                g.wait_ge(f_wblk.h, 2 * KC + m - 1)
                mark[f"wo{m}"] = dblk.issue(g.dma_start(
                    t_wkbuf[m % 2][:, :], d_wo[m], max_dma_last_dim=2048))

        def emit_proj_rope(name, wprefix, n_tok, dst, dst_stride, q_off, bank_wait_mark):
            sl = ts_slices(n_tok)
            S = len(sl)
            es_k = ExitStack()
            ps_p = [[es_k.enter_context(nc.psum_tensor(f"ps_{name}p{pp}{i}", [128, 512], F32))
                     for i in range(len(sl))] for pp in range(2)]
            ps_w = [es_k.enter_context(nc.psum_tensor(f"ps_{name}w{i}", [128, 512], F32))
                    for i in range(2)]
            t_base, s_base, v_base = st_t.n, st_s.n, st_v.n
            fb = f_wblk.n

            def kraw(m, o, n):
                return t_xb[m % 2][:, o:o + n]

            @block.tensor
            def _(t):
                t.wait_ge(st_v.h, mark[bank_wait_mark])
                t.wait_ge(st_v.h, mark[f"h{KC - 1}"])
                for m in range(KC + 1):
                    if m < KC:
                        WT(t, mark[f"{wprefix}{m}"])
                        if m >= 2:
                            t.wait_ge(st_s.h, s_base + m - 1)
                        last = None
                        for k in range(KC):
                            for i, (o, n) in enumerate(sl):
                                last = t.matmul(ps_p[m % 2][i][:, 0:n],
                                                t_wkbuf[m % 2][:, k * 128:(k + 1) * 128],
                                                hap(k, q_off + o, n),
                                                start=(k == 0), stop=(k == KC - 1))
                        f_wblk.inc(last)
                        assert f_wblk.n == fb + m + 1
                    mm = m - 1
                    if mm >= 0:
                        t.wait_ge(st_s.h, s_base + mm + 1)
                        for i, (o, n) in enumerate(sl):
                            w_idx = mm * S + i
                            if w_idx >= 2:
                                t.wait_ge(st_v.h, v_base + w_idx - 1)
                            st_t.inc(t.matmul(ps_w[w_idx % 2][:, 0:n], t_swap[:, :],
                                              kraw(mm, o, n), start=True, stop=True))
                            assert st_t.n == t_base + w_idx + 1

            @block.scalar
            def _(s):
                for m in range(KC):
                    s.wait_ge(f_wblk.h, fb + m + 1)
                    if m >= 2:
                        s.wait_ge(st_t.h, t_base + (m - 1) * S)
                        s.wait_ge(st_v.h, v_base + (m - 1) * S)
                    last = None
                    for i, (o, n) in enumerate(sl):
                        last = s.activation(out=kraw(m, o, n), in_=ps_p[m % 2][i][:, 0:n],
                                            func=AF.Copy)
                    st_s.inc(last)
                    assert st_s.n == s_base + m + 1

            @block.vector
            def _(v):
                for m in range(KC):
                    for i, (o, n) in enumerate(sl):
                        w_idx = m * S + i
                        v.wait_ge(st_t.h, t_base + w_idx + 1)
                        co = q_off + o
                        v.tensor_tensor(t_rsa[:, 0:n], kraw(m, o, n), t_cs1[:, co:co + n],
                                        OP.mult)
                        v.tensor_tensor(t_htmp[:, 0:n], ps_w[w_idx % 2][:, 0:n],
                                        t_cs2[:, co:co + n], OP.mult)
                        st_v.inc(v.tensor_tensor(
                            dst[:, m * dst_stride + o: m * dst_stride + o + n],
                            t_htmp[:, 0:n], t_rsa[:, 0:n], OP.add))
                        assert st_v.n == v_base + w_idx + 1
            mark[f"{name}_done"] = st_v.n
            es_k.close()

        emit_proj_rope("K", "wk", T_WIN, t_krot, T_WIN, 0, "mod0_done")
        if PHASES < 5:
            emit_tail()
            return nc
        emit_mod_pass(1, "wmodB", "K_done")
        if PHASES < 6:
            emit_tail()
            return nc
        emit_proj_rope("Q", "wq", T_OWN, t_qrot, T_OWN, 256, "mod1_done")
        if PHASES < 7:
            emit_tail()
            return nc
        emit_mod_pass(2, "wmodC", "Q_done")
        if PHASES < 8:
            emit_tail()
            return nc

        es_v = ExitStack()
        ps_v = [es_v.enter_context(nc.psum_tensor(f"ps_v{i}", [128, 512], F32)) for i in range(2)]
        vp_t_base, vp_s_base = st_t.n, st_s.n

        @block.tensor
        def _(t):
            t.wait_ge(st_v.h, mark["mod2_done"])
            idx = 0
            for np_ in range(4):
                WT(t, mark[f"wv{np_}a"])
                for tt in range(10):
                    if idx >= 2:
                        t.wait_ge(st_s.h, vp_s_base + idx - 1)
                    last = None
                    for k in range(KC):
                        if tt == 0 and k == 12:
                            WT(t, mark[f"wv{np_}"])
                        last = t.matmul(ps_v[idx % 2][:, :], hap(k, tt * 128, 128),
                                        wvsl(np_, k),
                                        start=(k == 0), stop=(k == KC - 1))
                    if tt == 9:
                        f_wv.inc(last)
                    else:
                        st_t.inc(last)
                    idx += 1

        @block.scalar
        def _(s):
            idx = 0
            for np_ in range(4):
                for tt in range(10):
                    if tt == 9:
                        s.wait_ge(f_wv.h, np_ + 1)
                    else:
                        s.wait_ge(st_t.h, vp_t_base + np_ * 9 + tt + 1)
                    st_s.inc(s.activation(
                        out=t_vtm[:, tt * D + np_ * 512: tt * D + (np_ + 1) * 512],
                        in_=ps_v[idx % 2][:, :], func=AF.Copy))
                    idx += 1
            mark["vtm_done"] = st_s.n
        es_v.close()

        if PHASES < 9:
            emit_tail()
            return nc

        @block.sync
        def _(y):
            y.wait_ge(st_v.h, mark["mod2_done"])
            for cls in range(2):
                s_msk.inc(y.dma_start(t_masks[:, cls * 1024:(cls + 1) * 1024], d_masks[cls]), 16)
            mark["masks"] = (s_msk.h, s_msk.n)

        es_a = ExitStack()
        ps_sc = es_a.enter_context(nc.psum_tensor("ps_sc", [128, 2048], F32))
        ps_at = [es_a.enter_context(nc.psum_tensor(f"ps_at{i}", [128, 1024], BF16))
                 for i in range(3)]
        ps_av = es_a.enter_context(nc.psum_tensor("ps_av", [128, 512], F32))
        NG = 32

        tval, vval, sval, eval_ = {}, {}, {}, {}
        tb, vb_, sb_, eb_ = st_t.n, st_v.n, st_s.n, s_ev.n
        tc, vc_, sc_, ec_ = tb, vb_, sb_, eb_
        for g in range(NG + 2):
            if g < NG:
                tc += 1
                tval[("sc", g)] = tc
            if 0 <= g - 1 < NG:
                tc += 1
                tval[("tr", g - 1)] = tc
            if 0 <= g - 2 < NG:
                tc += 1
                tval[("av", g - 2)] = tc
        for g in range(NG + 2):
            if g < NG:
                vc_ += 1
                vval[("mm", g)] = vc_
            if 0 <= g - 1 < NG:
                vc_ += 1
                vval[("rn", g - 1)] = vc_
            if 0 <= g - 2 < NG:
                vc_ += 1
                vval[("c0", g - 2)] = vc_
                vc_ += 1
                vval[("c1", g - 2)] = vc_
        for g in range(NG + 2):
            if g < NG:
                sc_ += 1
                sval[("ex", g)] = sc_
            if 0 <= g - 2 < NG:
                ec_ += 1
                eval_[("e0", g - 2)] = ec_
                ec_ += 1
                eval_[("e1", g - 2)] = ec_

        @block.tensor
        def _(t):
            t.wait_ge(st_s.h, mark["vtm_done"])
            t.wait_ge(st_v.h, mark["mod2_done"])
            for g in range(NG + 2):
                if g < NG:
                    ci, j2 = divmod(g, 8)
                    vb = ci * 256
                    if g >= 1:
                        t.wait_ge(st_v.h, vval[("mm", g - 1)])
                    last = None
                    for h in range(2):
                        hd = 2 * j2 + h
                        for wt in range(2):
                            qc = hd * T_OWN + ci * 256 + wt * 128
                            last = t.matmul(ps_sc[:, (h * 2 + wt) * 512:(h * 2 + wt + 1) * 512],
                                            t_qrot[:, qc:qc + 128],
                                            t_krot[:, hd * T_WIN + vb: hd * T_WIN + vb + 512],
                                            start=True, stop=True)
                    st_t.inc(last)
                    assert st_t.n == tval[("sc", g)]
                gg = g - 1
                if 0 <= gg < NG:
                    p = gg % 2
                    t.wait_ge(st_v.h, vval[("rn", gg)])
                    last = None
                    for h in range(2):
                        u = 2 * gg + h - 3
                        if u >= 0:
                            t.wait_ge(st_v.h, vval[(f"c{u % 2}", u // 2)])
                        dst = ps_at[(2 * gg + h) % 3]
                        for vc in range(4):
                            for wt in range(2):
                                last = t.transpose(
                                    dst[:, vc * 256 + wt * 128: vc * 256 + wt * 128 + 128],
                                    t_expn2[p][:, (h * 2 + wt) * 512 + vc * 128:
                                               (h * 2 + wt) * 512 + (vc + 1) * 128],
                                    t_idb[:, :])
                    st_t.inc(last)
                    assert st_t.n == tval[("tr", gg)]
                gg = g - 2
                if 0 <= gg < NG:
                    ci, j2 = divmod(gg, 8)
                    if ("e1", gg - 1) in eval_:
                        t.wait_ge(s_ev.h, eval_[("e1", gg - 1)])
                    last = None
                    for h in range(2):
                        hd = 2 * j2 + h
                        t.wait_ge(st_v.h, vval[(f"c{h}", gg)])
                        for vc in range(4):
                            tvi = ci * 2 + vc
                            last = t.matmul(ps_av[:, h * 256:(h + 1) * 256],
                                            t_vtm[:, tvi * D + hd * 128: tvi * D + (hd + 1) * 128],
                                            t_atsb2[h][:, vc * 256:(vc + 1) * 256],
                                            start=(vc == 0), stop=(vc == 3))
                    st_t.inc(last)
                    assert st_t.n == tval[("av", gg)]

        @block.vector
        def _(v):
            WT(v, mark["masks"])
            for g in range(NG + 2):
                if g < NG:
                    ci = g // 8
                    p = g % 2
                    cls = 0 if ci == 0 else 1
                    v.wait_ge(st_t.h, tval[("sc", g)])
                    if ("ex", g - 2) in sval:
                        v.wait_ge(st_s.h, sval[("ex", g - 2)])
                    mask3 = t_masks[:, cls * 1024:(cls + 1) * 1024].unsqueeze(1).broadcast_to(
                        [128, 2, 1024])
                    v.scalar_tensor_tensor(
                        out=t_msk2[p].rearrange("a (h n) -> a h n", h=2),
                        in0=ps_sc.ap().rearrange("a (h n) -> a h n", h=2),
                        scalar=SCALE, in1=mask3, op0=OP.mult, op1=OP.add)
                    st_v.inc(v.reduce_max(t_tiny[:, p * 12: p * 12 + 4],
                                          t_msk2[p].rearrange("a (q n) -> a q n", q=4),
                                          axis=AX.X, negate=True))
                    assert st_v.n == vval[("mm", g)]
                gg = g - 1
                if 0 <= gg < NG:
                    p = gg % 2
                    v.wait_ge(st_s.h, sval[("ex", gg)])
                    for q in range(4):
                        v.reciprocal(t_tiny[:, p * 12 + 8 + q: p * 12 + 9 + q],
                                     t_tiny[:, p * 12 + 4 + q: p * 12 + 5 + q])
                    last = None
                    for q in range(4):
                        last = v.tensor_scalar(out=t_expn2[p][:, q * 512:(q + 1) * 512],
                                               in0=t_expn2[p][:, q * 512:(q + 1) * 512],
                                               scalar1=t_tiny[:, p * 12 + 8 + q: p * 12 + 9 + q],
                                               scalar2=None, op0=OP.mult)
                    st_v.inc(last)
                    assert st_v.n == vval[("rn", gg)]
                gg = g - 2
                if 0 <= gg < NG:
                    for h in range(2):
                        v.wait_ge(st_t.h, tval[("tr", gg)])
                        if ("av", gg - 1) in tval:
                            v.wait_ge(st_t.h, tval[("av", gg - 1)])
                        st_v.inc(v.tensor_copy(t_atsb2[h][:, :], ps_at[(2 * gg + h) % 3][:, :]))
                        assert st_v.n == vval[(f"c{h}", gg)]

        @block.scalar
        def _(s):
            for g in range(NG + 2):
                if g < NG:
                    p = g % 2
                    s.wait_ge(st_v.h, vval[("mm", g)])
                    if ("tr", g - 2) in tval:
                        s.wait_ge(st_t.h, tval[("tr", g - 2)])
                    for q in range(4):
                        s.activation(out=t_expn2[p][:, q * 512:(q + 1) * 512],
                                     in_=t_msk2[p][:, q * 512:(q + 1) * 512],
                                     func=AF.Exp,
                                     bias=t_tiny[:, p * 12 + q: p * 12 + q + 1],
                                     scale=1.0,
                                     accum_out=t_tiny[:, p * 12 + 4 + q: p * 12 + 5 + q])
                    st_s.inc(s.activation(out=t_tiny[:, p * 12 + 4: p * 12 + 8],
                                          in_=t_tiny[:, p * 12 + 4: p * 12 + 8],
                                          func=AF.Copy))
                    assert st_s.n == sval[("ex", g)]
                gg = g - 2
                if 0 <= gg < NG:
                    ci, j2 = divmod(gg, 8)
                    s.wait_ge(st_t.h, tval[("av", gg)])
                    for h in range(2):
                        hd = 2 * j2 + h
                        s_ev.inc(s.activation(
                            out=t_attn[:, hd * T_OWN + ci * 256: hd * T_OWN + (ci + 1) * 256],
                            in_=ps_av[:, h * 256:(h + 1) * 256], func=AF.Copy))
                        assert s_ev.n == eval_[(f"e{h}", gg)]
        mark["attn_done"] = s_ev.n
        es_a.close()

        if PHASES < 10:
            emit_tail()
            return nc

        es_o = ExitStack()
        ps_o = [[es_o.enter_context(nc.psum_tensor(f"ps_o{p}{i}", [128, 512], F32))
                 for i in range(2)] for p in range(2)]
        ps_s2 = [es_o.enter_context(nc.psum_tensor(f"ps_s2_{i}", [1, 512], F32)) for i in range(2)]
        ps_q2 = [es_o.enter_context(nc.psum_tensor(f"ps_q2_{i}", [1, 512], F32)) for i in range(2)]
        wo_t_base, wo_v_base, wo_s_base = st_t.n, st_v.n, st_s.n
        wo_fb = f_wblk.n

        @block.tensor
        def _(t):
            t.wait_ge(s_ev.h, mark["attn_done"])
            for m in range(KC):
                WT(t, mark[f"wo{m}"])
                if m >= 2:
                    t.wait_ge(st_s.h, wo_s_base + 2 * (m - 2) + 1)
                p = m % 2
                last = None
                for k in range(KC):
                    for i in range(2):
                        last = t.matmul(ps_o[p][i][:, :],
                                        t_wkbuf[m % 2][:, k * 128:(k + 1) * 128],
                                        t_attn[:, k * T_OWN + i * 512: k * T_OWN + (i + 1) * 512],
                                        start=(k == 0), stop=(k == KC - 1))
                f_wblk.inc(last)
                if m >= 1:
                    t.wait_ge(st_v.h, wo_v_base + 2 * m)
                    t.wait_ge(st_s.h, wo_s_base + 2 * m)
                    last = None
                    for i in range(2):
                        t.matmul(ps_s2[i][:, :], t_onesb[:, :],
                                 t_x2b[:, i * 512:(i + 1) * 512],
                                 start=(m == 1), stop=False)
                        last = t.matmul(ps_q2[i][:, :], t_onesb[:, :],
                                        t_x2sq[:, i * 512:(i + 1) * 512],
                                        start=(m == 1), stop=False)
                    st_t.inc(last)
                else:
                    st_t.inc(t.nop())
                assert st_t.n == wo_t_base + m + 1
            t.wait_ge(st_v.h, wo_v_base + 2 * KC)
            t.wait_ge(st_s.h, wo_s_base + 2 * KC)
            last = None
            for i in range(2):
                t.matmul(ps_s2[i][:, :], t_onesb[:, :], t_x2b[:, i * 512:(i + 1) * 512],
                         start=False, stop=True)
                last = t.matmul(ps_q2[i][:, :], t_onesb[:, :], t_x2sq[:, i * 512:(i + 1) * 512],
                                start=False, stop=True)
            st_t.inc(last)
            mark["stats2_mm"] = st_t.n

        @block.scalar
        def _(s):
            for m in range(KC):
                p = m % 2
                s.wait_ge(f_wblk.h, wo_fb + m + 1)
                last = None
                for i in range(2):
                    last = s.activation(out=t_htmp[:, i * 512:(i + 1) * 512],
                                        in_=ps_o[p][i][:, :], func=AF.Copy, bias=0.0,
                                        scale=t_modt[:, 32 + m:33 + m])
                st_s.inc(last)
                s.wait_ge(st_v.h, wo_v_base + 2 * m + 1)
                st_s.inc(s.activation(out=t_x2sq[:, :], in_=t_x2buf[p][:, :], func=AF.Square))
                assert st_s.n == wo_s_base + 2 * m + 2

        @block.sync
        def _(y):
            for m in range(KC):
                if m >= 2:
                    y.wait_ge(st_v.h, wo_v_base + 2 * (m - 2) + 1)
                else:
                    y.wait_ge(s_ev.h, mark["attn_done"])
                mark[f"xre{m}"] = dxre.issue(y.dma_start(t_xrebuf[m % 2][:, :],
                                                         d_x[m][:, 256:T_WIN]))
                if m >= 1:
                    y.wait_ge(st_v.h, wo_v_base + 2 * (m - 1) + 1)
                    mark[f"x2sp{m - 1}"] = dsp.issue(
                        y.dma_start(d_x2[m - 1], t_x2buf[(m - 1) % 2][:, :]))
            y.wait_ge(st_v.h, wo_v_base + 2 * (KC - 1) + 1)
            mark[f"x2sp{KC - 1}"] = dsp.issue(
                y.dma_start(d_x2[KC - 1], t_x2buf[(KC - 1) % 2][:, :]))

        @block.vector
        def _(v):
            for m in range(KC):
                p = m % 2
                v.wait_ge(st_s.h, wo_s_base + 2 * m + 1)
                WT(v, mark[f"xre{m}"])
                if m >= 2:
                    WT(v, mark[f"x2sp{m - 2}"])
                inst = v.tensor_tensor(t_x2buf[p][:, :], t_htmp[:, 0:T_OWN],
                                       t_xrebuf[p][:, :], OP.add)
                st_v.inc(inst)
                assert st_v.n == wo_v_base + 2 * m + 1
                if m >= 1:
                    v.wait_ge(st_t.h, wo_t_base + m)
                st_v.inc(v.tensor_copy(t_x2b[:, :], t_x2buf[p][:, :]))
                assert st_v.n == wo_v_base + 2 * m + 2

        res15_v = wo_v_base + 2 * (KC - 1) + 1

        if PHASES < 11:
            emit_tail()
            return nc

        @block.vector
        def _(v):
            v.wait_ge(st_t.h, mark["stats2_mm"])
            for i in range(2):
                v.tensor_copy(t_rmu[:, i * 512:(i + 1) * 512], ps_s2[i][:, :])
                v.tensor_copy(t_rvar[:, i * 512:(i + 1) * 512], ps_q2[i][:, :])
            v.tensor_scalar(out=t_rmu[:, 0:T_OWN], in0=t_rmu[:, 0:T_OWN],
                            scalar1=1.0 / 2048.0, scalar2=None, op0=OP.mult)
            v.tensor_tensor(t_rstd[:, 0:T_OWN], t_rmu[:, 0:T_OWN], t_rmu[:, 0:T_OWN], OP.mult)
            st_v.inc(v.scalar_tensor_tensor(out=t_rvar[:, 0:T_OWN],
                                            in0=t_rvar[:, 0:T_OWN], scalar=1.0 / 2048.0,
                                            in1=t_rstd[:, 0:T_OWN],
                                            op0=OP.mult, op1=OP.subtract))
            mark["var2"] = st_v.n

        @block.scalar
        def _(s):
            s.wait_ge(st_v.h, mark["var2"])
            s.activation(out=t_rstd[:, 0:T_OWN], in_=t_rvar[:, 0:T_OWN], func=AF.Ln,
                         bias=t_cval[0:1, CV_EPS:CV_EPS + 1], scale=1.0)
            st_s.inc(s.activation(out=t_rstd[:, 0:T_OWN], in_=t_rstd[:, 0:T_OWN],
                                  func=AF.Exp, bias=0.0,
                                  scale=t_cval[0:1, CV_NHALF:CV_NHALF + 1]))
            mark["rstd2"] = st_s.n

        @block.vector
        def _(v):
            v.wait_ge(st_s.h, mark["rstd2"])
            st_v.inc(v.tensor_tensor(t_sdgb[:, :], t_sd[:, :], t_modt[:, 80:96], OP.mult))
            mark["b2"] = st_v.n
        es_o.close()

        es_b2 = ExitStack()
        ps_b2 = [es_b2.enter_context(nc.psum_tensor(f"ps_b2_{i}", [128, 512], F32))
                 for i in range(2)]
        b2_t_base, b2_v_base = st_t.n, st_v.n

        @block.tensor
        def _(t):
            t.wait_ge(st_v.h, mark["b2"])
            for j, row in enumerate([t_rstd, t_rmu]):
                for i in range(2):
                    idx = j * 2 + i
                    if idx >= 2:
                        t.wait_ge(st_v.h, b2_v_base + idx - 1)
                    st_t.inc(t.matmul(ps_b2[idx % 2][:, :], t_onesf[:, :],
                                      row[:, i * 512:(i + 1) * 512],
                                      start=True, stop=True))

        @block.vector
        def _(v):
            for i in range(2):
                v.wait_ge(st_t.h, b2_t_base + i + 1)
                st_v.inc(v.tensor_copy(t_rstd2b[:, i * 512:(i + 1) * 512], ps_b2[i % 2][:, :]))
            for i in range(2):
                idx = 2 + i
                v.wait_ge(st_t.h, b2_t_base + idx + 1)
                st_v.inc(v.scalar_tensor_tensor(out=t_b2b[:, i * 512:(i + 1) * 512],
                                                in0=ps_b2[idx % 2][:, :], scalar=-1.0,
                                                in1=t_rstd2b[:, i * 512:(i + 1) * 512],
                                                op0=OP.mult, op1=OP.mult))
            mark["bcast2"] = st_v.n
        es_b2.close()

        h2_v_base = st_v.n
        @block.sync
        def _(y):
            for c in range(KC):
                if c >= 2:
                    y.wait_ge(st_v.h, h2_v_base + c - 1)
                elif c == 1:
                    y.wait_ge(st_v.h, res15_v)
                WT(y, mark[f"x2sp{c}"])
                mark[f"x2re{c}"] = dx2re.issue(y.dma_start(t_x2re[c % 2][:, :], d_x2[c]))

        @block.vector
        def _(v):
            for c in range(KC):
                WT(v, mark[f"x2re{c}"])
                v.tensor_tensor(t_h2scr[:, :], t_x2re[c % 2][:, :], t_rstd2b[:, :], OP.mult)
                v.tensor_tensor(t_h2scr[:, :], t_h2scr[:, :], t_b2b[:, :], OP.add)
                inst = v.tensor_scalar(out=h2ap(c, 0, T_OWN), in0=t_h2scr[:, :],
                                       scalar1=t_modt[:, 64 + c:65 + c],
                                       scalar2=t_modt[:, 48 + c:49 + c],
                                       op0=OP.mult, op1=OP.add)
                st_v.inc(inst)
                mark[f"h2_{c}"] = st_v.n

        if PHASES < 12:
            emit_tail()
            return nc

        @block.gpsimd
        def _(g):
            for m in range(HC):
                if m >= 2:
                    g.wait_ge(f_wblk.h, 3 * KC + m - 1)
                else:
                    g.wait_ge(st_t.h, mark["stats2_mm"])
                    WT(g, mark["x2sp14"])
                    WT(g, mark["x2sp15"])
                mark[f"wu{m}"] = dblk.issue(g.dma_start(
                    t_wubuf[m % 2][:, :], d_wu[m], max_dma_last_dim=2048))
            if PHASES < 13:
                return
            for m in range(HC):
                g.wait_ge(f_wblk.h, 3 * KC + HC + m - 1)
                mark[f"wg{m}"] = dblk.issue(g.dma_start(
                    t_wubuf[m % 2][:, :], d_wg[m], max_dma_last_dim=2048))
            if PHASES < 14:
                return
            for m in range(KC):
                if m >= 2:
                    g.wait_ge(f_wd.h, m - 1)
                mark[f"wd{m}"] = dwd.issue(g.dma_start(
                    t_wdbuf[m % 2][:, :], d_wd[m], max_dma_last_dim=2048))

        def emit_ffn_ug(name, wprefix, is_gate, bank_wait_mark):
            es_u = ExitStack()
            ps_u = [[es_u.enter_context(nc.psum_tensor(f"ps_{name}{p}{i}", [128, 512], F32))
                     for i in range(2)] for p in range(2)]
            t_base, s_base, v_base = st_t.n, st_s.n, st_v.n
            fbase = f_wblk.n
            vper = 2 if is_gate else 1

            @block.vector
            def _(v):
                for m in range(HC):
                    WT(v, mark[f"{wprefix}{m}"])
                    st_v.inc(v.tensor_scalar(out=t_wubuf[m % 2][:, :], in0=t_wubuf[m % 2][:, :],
                                             scalar1=-128.0, scalar2=None, op0=OP.add))
                    if is_gate:
                        if m >= 1:
                            v.wait_ge(st_s.h, s_base + m)
                            last = None
                            for i in range(2):
                                last = v.tensor_tensor(upap(m - 1, i * 512, 512),
                                                       t_gtmp[(m - 1) % 2][:, i * 512:(i + 1) * 512],
                                                       upap(m - 1, i * 512, 512), OP.mult)
                            st_v.inc(last)
                        else:
                            st_v.inc(v.engine_nop())
                    assert st_v.n == v_base + vper * m + vper
                if is_gate:
                    v.wait_ge(st_s.h, s_base + HC)
                    last = None
                    for i in range(2):
                        last = v.tensor_tensor(upap(HC - 1, i * 512, 512),
                                               t_gtmp[(HC - 1) % 2][:, i * 512:(i + 1) * 512],
                                               upap(HC - 1, i * 512, 512), OP.mult)
                    st_v.inc(last)
                mark[f"{name}_vdone"] = st_v.n

            @block.tensor
            def _(t):
                if bank_wait_mark is not None:
                    t.wait_ge(st_v.h, mark[bank_wait_mark])
                if is_gate:
                    t.wait_ge(st_s.h, mark["up_done"])
                for m in range(HC):
                    t.wait_ge(st_v.h, v_base + vper * m + 1)
                    if m >= 2:
                        t.wait_ge(st_s.h, s_base + m - 1)
                    p = m % 2
                    last = None
                    for k in range(KC):
                        if m == 0:
                            t.wait_ge(st_v.h, mark[f"h2_{k}"])
                        for i in range(2):
                            last = t.matmul(ps_u[p][i][:, :],
                                            t_wubuf[m % 2][:, k * 128:(k + 1) * 128],
                                            h2ap(k, i * 512, 512),
                                            start=(k == 0), stop=(k == KC - 1))
                    f_wblk.inc(last)

            @block.scalar
            def _(s):
                for m in range(HC):
                    s.wait_ge(f_wblk.h, fbase + m + 1)
                    p = m % 2
                    last = None
                    for i in range(2):
                        if is_gate:
                            last = s.activation(out=t_gtmp[m % 2][:, i * 512:(i + 1) * 512],
                                                in_=ps_u[p][i][:, :], func=AF.Silu, bias=0.0,
                                                scale=t_sg[:, m:m + 1])
                        else:
                            last = s.activation(out=upap(m, i * 512, 512), in_=ps_u[p][i][:, :],
                                                func=AF.Copy, bias=0.0, scale=t_su[:, m:m + 1])
                    st_s.inc(last)
                    assert st_s.n == s_base + m + 1
            mark[f"{name}_done"] = st_s.n
            es_u.close()

        emit_ffn_ug("up", "wu", False, "bcast2")
        if PHASES < 13:
            emit_tail()
            return nc
        emit_ffn_ug("gate", "wg", True, None)
        if PHASES < 14:
            emit_tail()
            return nc

        es_d = ExitStack()
        ps_d = [[es_d.enter_context(nc.psum_tensor(f"ps_d{p}{i}", [128, 512], F32))
                 for i in range(2)] for p in range(2)]
        dn_t_base, dn_v_base, dn_s_base = st_t.n, st_v.n, st_s.n

        @block.sync
        def _(y):
            for m in range(KC):
                if m >= 2:
                    y.wait_ge(st_v.h, dn_v_base + 2 * (m - 1) + 2)
                else:
                    y.wait_ge(st_v.h, mark[f"h2_{14 + m}"])
                mark[f"x2rf{m}"] = dx2re.issue(y.dma_start(t_x2re[m % 2][:, :], d_x2[m]))
                if m >= 2:
                    y.wait_ge(st_v.h, dn_v_base + 2 * (m - 2) + 2)
                    mark[f"yout{m - 2}"] = dout.issue(
                        y.dma_start(d_y[m - 2], t_outbuf[(m - 2) % 2][:, :]))
            for m in (KC - 2, KC - 1):
                y.wait_ge(st_v.h, dn_v_base + 2 * m + 2)
                mark[f"yout{m}"] = dout.issue(y.dma_start(d_y[m], t_outbuf[m % 2][:, :]))
            y.wait_ge(dout.sems[0].h, dout.sems[0].n)
            y.wait_ge(dout.sems[1].h, dout.sems[1].n)

        @block.vector
        def _(v):
            v.wait_ge(st_s.h, mark["gate_done"])
            v.wait_ge(st_v.h, mark["gate_vdone"])
            for m in range(KC):
                WT(v, mark[f"wd{m}"])
                st_v.inc(v.tensor_scalar(out=t_wdbuf[m % 2][:, :], in0=t_wdbuf[m % 2][:, :],
                                         scalar1=-128.0, scalar2=None, op0=OP.add))
                v.wait_ge(st_s.h, dn_s_base + m + 1)
                WT(v, mark[f"x2rf{m}"])
                if m >= 2:
                    WT(v, mark[f"yout{m - 2}"])
                inst = v.tensor_tensor(t_outbuf[m % 2][:, :], t_dscr[:, :],
                                       t_x2re[m % 2][:, :], OP.add)
                st_v.inc(inst)
                assert st_v.n == dn_v_base + 2 * m + 2

        @block.tensor
        def _(t):
            for m in range(KC):
                t.wait_ge(st_v.h, dn_v_base + 2 * m + 1)
                if m >= 2:
                    t.wait_ge(st_s.h, dn_s_base + m - 1)
                p = m % 2
                last = None
                for k in range(HC):
                    for i in range(2):
                        last = t.matmul(ps_d[p][i][:, :],
                                        t_wdbuf[m % 2][:, k * 128:(k + 1) * 128],
                                        upap(k, i * 512, 512),
                                        start=(k == 0), stop=(k == HC - 1))
                f_wd.inc(last)

        @block.scalar
        def _(s):
            for m in range(KC):
                s.wait_ge(f_wd.h, m + 1)
                if m >= 1:
                    s.wait_ge(st_v.h, dn_v_base + 2 * (m - 1) + 2)
                p = m % 2
                last = None
                for i in range(2):
                    last = s.activation(out=t_dscr[:, i * 512:(i + 1) * 512],
                                        in_=ps_d[p][i][:, :], func=AF.Copy, bias=0.0,
                                        scale=t_sdgb[:, m:m + 1])
                st_s.inc(last)
                assert st_s.n == dn_s_base + m + 1


    return nc


def _stage_weights(Wq, Wk, Wv, Wo, wg, wu, wd, wmod):
    perm = np.concatenate([np.arange(0, 128, 2), np.arange(1, 128, 2)])
    perm_full = np.concatenate([h * 128 + perm for h in range(NH)])

    def blkT(Wp, mc, kc):
        return np.ascontiguousarray(
            Wp.reshape(mc, 128, kc, 128).transpose(0, 3, 2, 1)).reshape(mc, 128, kc * 128)

    wq_m = blkT(np.ascontiguousarray(Wq[perm_full]), KC, KC)
    wk_m = blkT(np.ascontiguousarray(Wk[perm_full]), KC, KC)
    wo_m = blkT(np.ascontiguousarray(Wo), KC, KC)
    wv_k = np.ascontiguousarray(Wv.T).reshape(KC, 128, D)
    def lo8(a):
        return np.ascontiguousarray(a.view(np.uint8)[..., 0::4])

    wg_m = lo8(blkT(np.ascontiguousarray(wg), HC, KC))
    wu_m = lo8(blkT(np.ascontiguousarray(wu), HC, KC))
    wd_m = lo8(blkT(np.ascontiguousarray(wd), KC, HC))
    wmod_k = np.ascontiguousarray(wmod).reshape(KC, 128, 6 * D)
    return wq_m, wk_m, wv_k, wo_m, wg_m, wu_m, wd_m, wmod_k


def _rope_tables(g):
    inv = 10000.0 ** (-(np.arange(64, dtype=np.float64) / 64.0))
    pos = g * 1024 - 256 + np.arange(T_WIN, dtype=np.float64)
    fr = np.outer(inv, pos)
    cos = np.cos(fr)
    sin = np.sin(fr)
    cs1 = np.concatenate([cos, cos], axis=0).astype(np.float32)
    cs2 = np.concatenate([-sin, sin], axis=0).astype(np.float32)
    return np.ascontiguousarray(cs1), np.ascontiguousarray(cs2)


def _masks(g):
    out = np.zeros((2, 128, 2, 512), np.float32)
    v = np.arange(512)
    for cls in range(2):
        for wt in range(2):
            for p2 in range(128):
                wrow = wt * 128 + p2
                lo = 256 if (cls == 0 and g == 0) else 0
                bad = (v < lo) | (v >= 257 + wrow)
                out[cls, p2, wt, bad] = NEG
    return np.ascontiguousarray(out.reshape(2, 128, 1024))


def kernel(L0_attn_x, L0_attn_Wq, L0_attn_Wk, L0_attn_Wv, L0_attn_Wo,
           L0_ffn_wq_gate, L0_ffn_s_gate, L0_ffn_wq_up, L0_ffn_s_up,
           L0_ffn_wq_down, L0_ffn_s_down, L0_c, L0_wmod):
    x = np.asarray(L0_attn_x, np.float32)
    wq_m, wk_m, wv_k, wo_m, wg_m, wu_m, wd_m, wmod_k = _stage_weights(
        np.asarray(L0_attn_Wq, np.float32), np.asarray(L0_attn_Wk, np.float32),
        np.asarray(L0_attn_Wv, np.float32), np.asarray(L0_attn_Wo, np.float32),
        np.asarray(L0_ffn_wq_gate, np.int32), np.asarray(L0_ffn_wq_up, np.int32),
        np.asarray(L0_ffn_wq_down, np.int32), np.asarray(L0_wmod, np.float32))

    s_gate_t = np.ascontiguousarray(np.asarray(L0_ffn_s_gate, np.float32).reshape(HC, 128).T)
    s_up_t = np.ascontiguousarray(np.asarray(L0_ffn_s_up, np.float32).reshape(HC, 128).T)
    s_down_t = np.ascontiguousarray(np.asarray(L0_ffn_s_down, np.float32).reshape(KC, 128).T)

    id_bf = np.eye(128, dtype=ml_dtypes.bfloat16)
    id_f = np.eye(128, dtype=np.float32)
    swap = np.zeros((128, 128), ml_dtypes.bfloat16)
    swap[(np.arange(128) + 64) % 128, np.arange(128)] = 1
    ones_b = np.ones((128, 1), ml_dtypes.bfloat16)
    ones_f = np.ones((1, 128), np.float32)
    cvals = np.zeros((128, 8), np.float32)
    cvals[:, CV_N128] = -128.0
    cvals[:, CV_EPS] = 1e-5
    cvals[:, CV_NHALF] = -0.5

    xT = np.ascontiguousarray(x.transpose(0, 2, 1))

    in_maps = []
    for core in range(8):
        b, g = divmod(core, 2)
        if g == 0:
            xc = np.concatenate([xT[b][:, 0:256], xT[b][:, 0:1024]], axis=1)
        else:
            xc = xT[b][:, 768:2048]
        x_in = np.ascontiguousarray(xc).reshape(KC, 128, T_WIN)
        c_col = np.asarray(L0_c, np.float32)[b].reshape(KC, 128).T
        c_rep = np.ascontiguousarray(np.repeat(c_col[:, :, None], 4, axis=2).reshape(128, 64))
        cs1, cs2 = _rope_tables(g)
        msk = _masks(g)
        in_maps.append(dict(
            x_in=x_in, wq_m=wq_m, wk_m=wk_m, wv_k=wv_k, wo_m=wo_m,
            wg_m=wg_m, wu_m=wu_m, wd_m=wd_m, wmod_k=wmod_k, c_rep=c_rep,
            s_gate_t=s_gate_t, s_up_t=s_up_t, s_down_t=s_down_t,
            cs1=cs1, cs2=cs2, masks=msk, id_bf=id_bf, id_f=id_f,
            swap_bf=swap, ones_b=ones_b, ones_f=ones_f, cvals=cvals,
        ))

    nc = build_nc()
    res = run_bass_kernel_spmd(nc, in_maps, core_ids=list(range(8)))

    out = np.empty((4, D, D), np.float32)
    for core in range(8):
        b, g = divmod(core, 2)
        yc = np.asarray(res.results[core]["y"]).reshape(D, T_OWN)
        out[b][:, g * 1024:(g + 1) * 1024] = yc
    return np.ascontiguousarray(out.transpose(0, 2, 1))
